# version 15
# baseline (speedup 1.0000x reference)
import numpy as np
from contextlib import ExitStack

import concourse.bass as bass
import concourse.mybir as mybir
from concourse.bass_utils import run_bass_kernel_spmd

F32 = mybir.dt.float32
BF16 = mybir.dt.bfloat16
AF = mybir.ActivationFunctionType
ALU = mybir.AluOpType
AX = mybir.AxisListType

P = 128
R = 2048
D = 1024
F = 4096
DT = D // P
FT = F // P
ET = D // P
RT = R // P
RB = 256
NRB = R // RB
EPS = 1e-5
N_CORES = 8


class Sem:
    def __init__(self, nc, ctx, name):
        self.h = ctx.enter_context(nc.semaphore(name))
        self.n = 0

    def inc(self, instr, k=1):
        instr.then_inc(self.h, k)
        self.n += k
        return self.n

    def tok(self, instr, k=16):
        return (self, self.inc(instr, k))


def wdma(eng, tok):
    eng.wait_ge(tok[0].h, tok[1])


def emit_ln_acc(scalar, xsrc, i, junk, ssum, ssq, sA):
    for rt in range(2):
        g = 2 * i + rt
        xt = xsrc[:, rt * D:(rt + 1) * D]
        n = sA.inc(scalar.activation(junk[:], xt, AF.Square,
                                     accum_out=ssq[:, g:g + 1]))
        scalar.wait_ge(sA.h, n)
        n = sA.inc(scalar.activation(junk[:], xt, AF.Copy,
                                     accum_out=ssum[:, g:g + 1]))
        scalar.wait_ge(sA.h, n)
    return sA.inc(scalar.drain())


def emit_ln_comb(v, i, acc_tok, ssum, ssq, mean, msq, vp, sV, sA):
    v.wait_ge(sA.h, acc_tok)
    n = 0
    for rt in range(2):
        g = 2 * i + rt
        n = sV.inc(v.tensor_scalar(mean[:, g:g + 1], ssum[:, g:g + 1],
                                   1.0 / 1024.0, None, ALU.mult))
        v.wait_ge(sV.h, n)
        n = sV.inc(v.scalar_tensor_tensor(msq[:, g:g + 1], mean[:, g:g + 1],
                                          1.0, mean[:, g:g + 1],
                                          ALU.mult, ALU.mult))
        v.wait_ge(sV.h, n)
        n = sV.inc(v.scalar_tensor_tensor(vp[:, g:g + 1], ssq[:, g:g + 1],
                                          1.0 / 1024.0, msq[:, g:g + 1],
                                          ALU.mult, ALU.subtract))
        v.wait_ge(sV.h, n)
    return n


def emit_ffn(nc, name, src_hbm, dst_hbm, wq1t_v, wq2t_v, wq1_ready, wq2_chunks,
             src_ready, ones_row_bf, b1h, b2h_bf, ident_bf, eps_c,
             sL, sS, sT, sV, sA, sG, final_wait=False, pre_body=None,
             st_ready=None):
    with ExitStack() as c:
        SB = lambda nm, sh, dt: c.enter_context(
            nc.sbuf_tensor(f"{name}_{nm}", sh, dt))
        xrb = [SB(f"x{i}", [P, 2 * D], F32) for i in range(2)]
        xn = [SB(f"xn{i}", [P, 2 * D], BF16) for i in range(2)]
        xnT = [SB(f"xnT{i}", [P, DT * RB], BF16) for i in range(2)]
        hT = SB("hT", [P, FT * RB], BF16)
        ssum = SB("ssum", [P, 2 * NRB], F32)
        ssq = SB("ssq", [P, 2 * NRB], F32)
        mean = SB("mean", [P, 2 * NRB], F32)
        msq = SB("msq", [P, 2 * NRB], F32)
        vp = SB("vp", [P, 2 * NRB], F32)
        rst = SB("rst", [P, 2 * NRB], F32)
        rsti = SB("rsti", [P, 2 * NRB], F32)
        sqg = SB("sqg", [P, D], F32)
        ps_h = [c.enter_context(nc.psum_tensor(f"{name}_psh{i}", [P, RB], F32))
                for i in range(4)]
        ps_oA = c.enter_context(nc.psum_tensor(f"{name}_psoA", [P, 512], F32))
        ps_oB = c.enter_context(nc.psum_tensor(f"{name}_psoB", [P, 512], F32))
        ps_x = c.enter_context(nc.psum_tensor(f"{name}_psx", [P, 2 * D], BF16))

        ld_x = [None] * NRB
        lnA = [0] * NRB
        sqd = [0] * NRB
        xnd = [0] * NRB
        trd = [0] * NRB
        xnTd = [0] * NRB
        h_mm = [0] * (NRB * FT)
        h_ev = [0] * (NRB * FT)
        o_mmA = [[0, 0] for _ in range(NRB)]
        o_mmB = [[0, 0] for _ in range(NRB)]
        st_done = [None] * NRB
        last_res = [0, 0]

        with nc.Block(name) as blk:
            if pre_body is not None:
                pre_body(blk)

            def load_x(sync, i):
                for tok in (src_ready or []):
                    wdma(sync, tok)
                if i >= 2:
                    wdma(sync, st_done[i - 2])
                src = src_hbm[i * RB:(i + 1) * RB, :] \
                    .rearrange("(t p) d -> p t d", p=P)
                ins = sync.dma_start(
                    out=xrb[i % 2][:].rearrange("p (t d) -> p t d", t=2),
                    in_=src)
                ld_x[i] = sL[i % 2].tok(ins)

            def s_x01(sync):
                load_x(sync, 0)
                load_x(sync, 1)
            blk.sync(s_x01)

            acc_tok = [0] * NRB

            def a_acc(scalar, i):
                wdma(scalar, ld_x[i])
                acc_tok[i] = emit_ln_acc(scalar, xrb[i % 2][:], i, sqg,
                                         ssum, ssq, sA)

            def v_comb(vector, i):
                lnA[i] = emit_ln_comb(vector, i, acc_tok[i], ssum, ssq,
                                      mean, msq, vp, sV, sA)

            def a_sqrt(scalar, i):
                scalar.wait_ge(sV.h, lnA[i])
                ins = scalar.activation(rst[:, 2 * i:2 * i + 2],
                                        vp[:, 2 * i:2 * i + 2], AF.Sqrt,
                                        bias=eps_c[:, 0:1], scale=4.0)
                sqd[i] = sA.inc(ins)

            def v_xn(vector, i):
                vector.wait_ge(sA.h, sqd[i])
                n = sV.inc(vector.reciprocal(rsti[:, 2 * i:2 * i + 2],
                                             rst[:, 2 * i:2 * i + 2]))
                vector.wait_ge(sV.h, n)
                if i >= 2:
                    vector.wait_ge(sT.h, trd[i - 2])
                ins = None
                for rt in range(2):
                    g = 2 * i + rt
                    ins = vector.tensor_scalar(
                        xn[i % 2][:, rt * D:(rt + 1) * D],
                        xrb[i % 2][:, rt * D:(rt + 1) * D],
                        mean[:, g:g + 1], rsti[:, g:g + 1],
                        ALU.subtract, ALU.mult)
                xnd[i] = sV.inc(ins)

            def t_tr(tensor, i):
                tensor.wait_ge(sV.h, xnd[i])
                if i >= 1:
                    tensor.wait_ge(sA.h, xnTd[i - 1])
                for rt in range(2):
                    for dt in range(DT):
                        tensor.transpose(
                            ps_x[:, rt * D + dt * P: rt * D + (dt + 1) * P],
                            xn[i % 2][:, rt * D + dt * P: rt * D + (dt + 1) * P],
                            ident_bf[:])
                trd[i] = sT.inc(tensor.drain())

            def a_xnT(scalar, i):
                scalar.wait_ge(sT.h, trd[i])
                dst = xnT[i % 2][:].rearrange("p (dt r) -> p dt r", dt=DT) \
                    .rearrange("p dt (rt c) -> p dt rt c", rt=2)
                src = ps_x[:].rearrange("p (rt dt c) -> p dt rt c", rt=2, dt=DT)
                xnTd[i] = sA.inc(scalar.copy(dst, src))

            blk.scalar(lambda sc: a_acc(sc, 0))
            blk.vector(lambda v: v_comb(v, 0))
            blk.scalar(lambda sc: a_sqrt(sc, 0))
            blk.vector(lambda v: v_xn(v, 0))
            blk.tensor(lambda t: t_tr(t, 0))
            blk.scalar(lambda sc: a_xnT(sc, 0))

            for i in range(NRB):
                if i + 1 < NRB:
                    blk.scalar(lambda sc, i=i: a_acc(sc, i + 1))

                vb = sV.n
                ln_incs = (6, 2) if i + 1 < NRB else (0, 0)
                pred_hev = [0] * FT
                for fc in range(16):
                    pred_hev[fc] = vb + 1 + fc
                for fc in range(16, 24):
                    pred_hev[fc] = vb + 16 + ln_incs[0] + (fc - 16) + 1
                for fc in range(24, FT):
                    pred_hev[fc] = vb + 24 + sum(ln_incs) + (fc - 24) + 1

                def t_mm1(tensor, i=i, pred_hev=pred_hev):
                    for fc in range(FT):
                        g = i * FT + fc
                        if fc == 0:
                            tensor.wait_ge(sA.h, xnTd[i])
                            if i == 0 and wq1_ready is not None:
                                wdma(tensor, wq1_ready)
                        if fc >= 4:
                            tensor.wait_ge(sV.h, pred_hev[fc - 4])
                        elif g >= 4:
                            tensor.wait_ge(sV.h, h_ev[g - 4])
                        ins = None
                        for dt in range(DT):
                            ins = tensor.matmul(
                                ps_h[fc % 4][:],
                                wq1t_v[:, dt * F + fc * P: dt * F + (fc + 1) * P],
                                xnT[i % 2][:, dt * RB:(dt + 1) * RB],
                                start=(dt == 0), stop=(dt == DT - 1))
                        h_mm[g] = sT.inc(ins)
                blk.tensor(t_mm1)

                def v_ev_range(vector, i, lo, hi):
                    for fc in range(lo, hi):
                        g = i * FT + fc
                        vector.wait_ge(sT.h, h_mm[g])
                        ins = vector.tensor_scalar(
                            hT[:, fc * RB:(fc + 1) * RB],
                            ps_h[fc % 4][:], b1h[:, fc:fc + 1], 0.0,
                            ALU.add, ALU.max)
                        h_ev[g] = sV.inc(ins)
                blk.vector(lambda v, i=i: v_ev_range(v, i, 0, 16))
                if i + 1 < NRB:
                    blk.vector(lambda v, i=i: v_comb(v, i + 1))
                    blk.scalar(lambda sc, i=i: a_sqrt(sc, i + 1))
                blk.vector(lambda v, i=i: v_ev_range(v, i, 16, 24))
                if i + 1 < NRB:
                    blk.vector(lambda v, i=i: v_xn(v, i + 1))
                blk.vector(lambda v, i=i: v_ev_range(v, i, 24, FT))
                assert sV.n == pred_hev[FT - 1], "mm1 evac prediction drift"

                if i + 1 < NRB:
                    blk.tensor(lambda t, i=i: t_tr(t, i + 1))
                    blk.scalar(lambda sc, i=i: a_xnT(sc, i + 1))

                def t_mm2(tensor, i=i):
                    v0 = sV.n
                    pred = [v0 + 1, v0 + 2, v0 + 3, v0 + 4]
                    tensor.wait_ge(sV.h, h_ev[i * FT + FT - 1])
                    for rt in range(2):
                        if rt == 0:
                            if last_res[0]:
                                tensor.wait_ge(sV.h, last_res[0])
                            if last_res[1]:
                                tensor.wait_ge(sV.h, last_res[1])
                        else:
                            tensor.wait_ge(sV.h, pred[0])
                            tensor.wait_ge(sV.h, pred[1])
                        for fc in range(FT):
                            if wq2_chunks is not None and i == 0 and rt == 0 \
                                    and fc % 8 == 0:
                                wdma(tensor, wq2_chunks[fc // 8])
                            lhs = hT[:, fc * RB + rt * P: fc * RB + (rt + 1) * P]
                            tensor.matmul(
                                ps_oA[:], lhs,
                                wq2t_v[:, fc * D: fc * D + 512],
                                start=(fc == 0), stop=False)
                            tensor.matmul(
                                ps_oB[:], lhs,
                                wq2t_v[:, fc * D + 512: fc * D + 1024],
                                start=(fc == 0), stop=False)
                        ins = tensor.matmul(ps_oA[:], ones_row_bf[:],
                                            b2h_bf[0:1, 0:512],
                                            start=False, stop=True)
                        o_mmA[i][rt] = sT.inc(ins)
                        ins = tensor.matmul(ps_oB[:], ones_row_bf[:],
                                            b2h_bf[0:1, 512:1024],
                                            start=False, stop=True)
                        o_mmB[i][rt] = sT.inc(ins)
                    last_res[0] = pred[2]
                    last_res[1] = pred[3]
                blk.tensor(t_mm2)

                def v_res(vector, i=i):
                    for rt in range(2):
                        vector.wait_ge(sT.h, o_mmA[i][rt])
                        base = rt * D
                        sV.inc(vector.tensor_tensor(
                            xrb[i % 2][:, base:base + 512],
                            xrb[i % 2][:, base:base + 512], ps_oA[:], ALU.add))
                        vector.wait_ge(sT.h, o_mmB[i][rt])
                        sV.inc(vector.tensor_tensor(
                            xrb[i % 2][:, base + 512:base + 1024],
                            xrb[i % 2][:, base + 512:base + 1024], ps_oB[:],
                            ALU.add))
                blk.vector(v_res)
                assert sV.n == last_res[1], "res token prediction drifted"

                def a_store(scalar, i=i):
                    scalar.wait_ge(sV.h, last_res[1])
                    if i < 2 and st_ready is not None:
                        wdma(scalar, st_ready[i])
                    dst = dst_hbm[i * RB:(i + 1) * RB, :] \
                        .rearrange("(t p) d -> p t d", p=P)
                    ins = scalar.dma_start(
                        out=dst,
                        in_=xrb[i % 2][:].rearrange("p (t d) -> p t d", t=2))
                    st_done[i] = sS[i % 2].tok(ins)
                blk.scalar(a_store)

                if i + 2 < NRB:
                    blk.sync(lambda sy, i=i: load_x(sy, i + 2))

            if final_wait:
                fa, fb = sS[0].n, sS[1].n

                def a_final(scalar):
                    scalar.wait_ge(sS[0].h, fa)
                    scalar.wait_ge(sS[1].h, fb)
                blk.scalar(a_final)


def build_program(last_phase=5):
    nc = bass.Bass("TRN2", target_bir_lowering=False)

    x_in = nc.dram_tensor("x", [R, D], F32, kind="ExternalInput").ap()
    wq_in = nc.dram_tensor("wq", [D, D], F32, kind="ExternalInput").ap()
    wk_in = nc.dram_tensor("wk", [D, D], F32, kind="ExternalInput").ap()
    wv_in = nc.dram_tensor("wv", [D, D], F32, kind="ExternalInput").ap()
    w1_in = nc.dram_tensor("w1", [F, D], F32, kind="ExternalInput").ap()
    b1_in = nc.dram_tensor("b1", [F], F32, kind="ExternalInput").ap()
    w2_in = nc.dram_tensor("w2", [D, F], F32, kind="ExternalInput").ap()
    b2_in = nc.dram_tensor("b2", [D], F32, kind="ExternalInput").ap()
    out_hbm = nc.dram_tensor("out", [R, D], F32, kind="ExternalOutput").ap()

    wq1t_hbm = nc.dram_tensor("wq1t_s", [D, F], BF16, kind="Internal").ap()
    wq2t_hbm = nc.dram_tensor("wq2t_s", [F, D], BF16, kind="Internal").ap()
    x1_hbm = nc.dram_tensor("x1_s", [R, D], F32, kind="Internal").ap()
    x2_hbm = nc.dram_tensor("x2_s", [R, D], F32, kind="Internal").ap()

    with ExitStack() as ctx:
        SB = lambda name, shape, dt: ctx.enter_context(
            nc.sbuf_tensor(name, shape, dt))

        ones_col = SB("ones_col", [P, 1], F32)
        ones_row = SB("ones_row", [1, P], F32)
        ones_row_bf = SB("ones_row_bf", [1, P], BF16)
        ident_bf = SB("ident_bf", [P, P], BF16)
        nident_bf = SB("nident_bf", [P, P], BF16)
        ident_f32 = SB("ident_f32", [P, P], F32)
        b1h = SB("b1h", [P, FT], F32)
        b2h_bf = SB("b2h_bf", [1, D], BF16)
        s12b = SB("s12b", [P, 2], F32)
        eps_c = SB("eps_c", [P, 2], F32)

        sL = [Sem(nc, ctx, f"sL{i}") for i in range(4)]
        sS = [Sem(nc, ctx, f"sS{i}") for i in range(2)]
        sC = [Sem(nc, ctx, f"sC{i}") for i in range(2)]
        sT = Sem(nc, ctx, "sT")
        sV = Sem(nc, ctx, "sV")
        sA = Sem(nc, ctx, "sA")
        sG = Sem(nc, ctx, "sG")
        sW = Sem(nc, ctx, "sW")

        with ExitStack() as c0:
            gstage = [c0.enter_context(nc.sbuf_tensor(f"gst{i}", [P, 8192], F32))
                      for i in range(2)]
            gacc = c0.enter_context(nc.sbuf_tensor("gacc", [P, 8], F32))
            gtot = c0.enter_context(nc.sbuf_tensor("gtot", [P, 2], F32))
            sinv = c0.enter_context(nc.sbuf_tensor("sinv", [1, 2], F32))
            b1_t32 = c0.enter_context(nc.sbuf_tensor("b1_t32", [32, P], F32))
            b2raw = c0.enter_context(nc.sbuf_tensor("b2raw", [1, D], F32))
            ps0 = c0.enter_context(nc.psum_tensor("ps0", [P, 2], F32))
            ps0b = c0.enter_context(nc.psum_tensor("ps0b", [P, 32], F32))

            w1_c = w1_in.rearrange("(c t p) d -> c p t d", p=P, t=8)
            w2_c = w2_in.rearrange("(c t p) f -> c p t f", p=P, t=2)

            ld_chunk = [None] * 8
            ld_b = [None, None]
            red_done = [0] * 8

            with nc.Block("p0") as blk:
                def load_chunk(sync, k):
                    if k < 4:
                        src = w1_c[k]
                        dst = gstage[k % 2][:].rearrange("p (t d) -> p t d", t=8)
                    else:
                        src = w2_c[k - 4]
                        dst = gstage[k % 2][:].rearrange("p (t f) -> p t f", t=2)
                    ins = sync.dma_start(out=dst, in_=src)
                    ld_chunk[k] = sL[k % 2].tok(ins)

                def s_first(sync):
                    load_chunk(sync, 0)
                    load_chunk(sync, 1)
                    ins = sync.dma_start(
                        out=b1_t32[:], in_=b1_in.rearrange("(c p) -> c p", p=P))
                    ld_b[0] = sL[2].tok(ins)
                    ins = sync.dma_start(out=b2raw[:], in_=b2_in[None, :])
                    ld_b[1] = sL[3].tok(ins)
                blk.sync(s_first)

                def gp_body(gpsimd):
                    gpsimd.memset(ones_col[:], 1.0)
                    gpsimd.memset(ones_row[:], 1.0)
                    gpsimd.memset(ones_row_bf[:], 1.0)
                    gpsimd.memset(eps_c[:, 0:1], 4.0 * EPS)
                    gpsimd.memset(eps_c[:, 1:2], EPS)
                    gpsimd.memset(ident_bf[:], 0.0)
                    gpsimd.memset(nident_bf[:], 0.0)
                    n = sG.inc(gpsimd.memset(ident_f32[:], 0.0))
                    gpsimd.wait_ge(sG.h, n)
                    gpsimd.affine_select(
                        out=ident_bf[:], in_=ident_bf[:],
                        compare_op=ALU.not_equal, fill=1.0, base=0,
                        pattern=[[-1, P]], channel_multiplier=1)
                    gpsimd.affine_select(
                        out=nident_bf[:], in_=nident_bf[:],
                        compare_op=ALU.not_equal, fill=-1.0, base=0,
                        pattern=[[-1, P]], channel_multiplier=1)
                    gpsimd.affine_select(
                        out=ident_f32[:], in_=ident_f32[:],
                        compare_op=ALU.not_equal, fill=1.0, base=0,
                        pattern=[[-1, P]], channel_multiplier=1)
                    sG.inc(gpsimd.drain())
                blk.gpsimd(gp_body)
                g_ready = sG.n

                for k in range(8):
                    def v_red(vector, k=k):
                        wdma(vector, ld_chunk[k])
                        ins = vector.reduce_sum(gacc[:, k:k + 1],
                                                gstage[k % 2][:], axis=AX.X,
                                                apply_absolute_value=True)
                        red_done[k] = sV.inc(ins)
                    blk.vector(v_red)
                    if k + 2 < 8:
                        def s_pref(sync, k=k):
                            sync.wait_ge(sV.h, red_done[k])
                            load_chunk(sync, k + 2)
                        blk.sync(s_pref)

                def v_tot(vector):
                    vector.wait_ge(sV.h, red_done[7])
                    vector.reduce_sum(gtot[:, 0:1], gacc[:, 0:4], axis=AX.X)
                    sV.inc(vector.reduce_sum(gtot[:, 1:2], gacc[:, 4:8],
                                             axis=AX.X))
                blk.vector(v_tot)
                v_tot_n = sV.n

                def t_sum(tensor):
                    tensor.wait_ge(sV.h, v_tot_n)
                    tensor.wait_ge(sG.h, g_ready)
                    ins = tensor.matmul(ps0[0:1, 0:2], ones_col[:], gtot[:],
                                        start=True, stop=True)
                    sT.inc(ins)
                blk.tensor(t_sum)
                t_sum_n = sT.n

                def v_sinv(vector):
                    vector.wait_ge(sT.h, t_sum_n)
                    n = sV.inc(vector.tensor_copy(sinv[:], ps0[0:1, 0:2]))
                    vector.wait_ge(sV.h, n)
                    n = sV.inc(vector.tensor_scalar(
                        sinv[:], sinv[:], 1.0 / 4194304.0, EPS,
                        ALU.mult, ALU.add))
                    vector.wait_ge(sV.h, n)
                    sV.inc(vector.reciprocal(sinv[:], sinv[:]))
                blk.vector(v_sinv)
                v_sinv_n = sV.n

                def t_bc(tensor):
                    tensor.wait_ge(sV.h, v_sinv_n)
                    tensor.matmul(ps0[:, 0:2], ones_row[:], sinv[:],
                                  start=True, stop=True)
                    wdma(tensor, ld_b[0])
                    tensor.transpose(ps0b[:, 0:32], b1_t32[:],
                                     ident_f32[0:32, 0:32])
                    sT.inc(tensor.drain())
                blk.tensor(t_bc)
                t_bc_n = sT.n

                def v_fin(vector):
                    vector.wait_ge(sT.h, t_bc_n)
                    vector.tensor_copy(s12b[:], ps0[:, 0:2])
                    vector.tensor_scalar(b1h[:], ps0b[:, 0:32], 0.5, None,
                                         ALU.mult)
                    wdma(vector, ld_b[1])
                    sV.inc(vector.tensor_scalar(b2h_bf[:], b2raw[:], 0.5, None,
                                                ALU.mult))
                blk.vector(v_fin)

        if last_phase < 1:
            return nc
        with ExitStack() as c12:
            wq1t = c12.enter_context(nc.sbuf_tensor("wq1t", [P, DT * F], BF16))
            wq2t = c12.enter_context(nc.sbuf_tensor("wq2t", [P, FT * D], BF16))
            wq1t_vv = wq1t[:].rearrange("p (dt f) -> p dt f", dt=DT)
            wq2t_vv = wq2t[:].rearrange("p (ft d) -> p ft d", ft=FT)

            with ExitStack() as c1:
                wstage = [c1.enter_context(nc.sbuf_tensor(f"wst{i}", [P, 1024], F32))
                          for i in range(2)]
                a_sb = [c1.enter_context(nc.sbuf_tensor(f"qa{i}", [P, 1024], BF16))
                        for i in range(2)]
                b_sb = [c1.enter_context(nc.sbuf_tensor(f"qb{i}", [P, 1024], BF16))
                        for i in range(2)]
                pswf = [c1.enter_context(nc.psum_tensor(f"pswf{i}", [P, 1024], F32))
                        for i in range(2)]

                NTILES = 64
                ld_w = [None] * NTILES
                ab_done = [0] * NTILES
                tr_done = [0] * NTILES
                ev_done = [0] * NTILES

                def w_src(t):
                    if t < 32:
                        return w1_in[t * P:(t + 1) * P, :]
                    u = t - 32
                    dt, fs = u // 4, u % 4
                    return w2_in[dt * P:(dt + 1) * P, fs * 1024:(fs + 1) * 1024]

                with nc.Block("p1") as blk:
                    def load_w(sync, t):
                        if t >= 2:
                            sync.wait_ge(sV.h, ab_done[t - 2])
                        ins = sync.dma_start(out=wstage[t % 2][:], in_=w_src(t))
                        ld_w[t] = sL[t % 2].tok(ins)

                    def s_first(sync):
                        load_w(sync, 0)
                        load_w(sync, 1)
                    blk.sync(s_first)

                    for t in range(NTILES):
                        sc = s12b[:, 0:1] if t < 32 else s12b[:, 1:2]

                        def v_ab(vector, t=t, sc=sc):
                            wdma(vector, ld_w[t])
                            if t >= 2:
                                vector.wait_ge(sT.h, tr_done[t - 2])
                            vector.tensor_scalar(
                                a_sb[t % 2][:], wstage[t % 2][:], sc, 0.5,
                                ALU.mult, ALU.is_ge)
                            ab_done[t] = sV.inc(vector.tensor_scalar(
                                b_sb[t % 2][:], wstage[t % 2][:], sc, -0.5,
                                ALU.mult, ALU.is_le))
                        blk.vector(v_ab)

                        def t_q(tensor, t=t):
                            tensor.wait_ge(sV.h, ab_done[t])
                            if t >= 2:
                                tensor.wait_ge(sA.h, ev_done[t - 2])
                            ins = None
                            for j in range(8):
                                sl = pswf[t % 2][:, j * P:(j + 1) * P]
                                tensor.matmul(sl, a_sb[t % 2][:, j * P:(j + 1) * P],
                                              ident_bf[:], start=True, stop=False)
                                ins = tensor.matmul(
                                    sl, b_sb[t % 2][:, j * P:(j + 1) * P],
                                    nident_bf[:], start=False, stop=True)
                            tr_done[t] = sT.inc(ins)
                        blk.tensor(t_q)

                        if t < 32:
                            dest = wq1t_vv[:, :, t * P:(t + 1) * P]
                        else:
                            u = t - 32
                            dt, fs = u // 4, u % 4
                            dest = wq2t_vv[:, fs * 8:(fs + 1) * 8,
                                           dt * P:(dt + 1) * P]
                        src = pswf[t % 2][:].rearrange("p (j c) -> p j c", j=8)

                        def a_ev(scalar, t=t, dest=dest, src=src):
                            scalar.wait_ge(sT.h, tr_done[t])
                            ev_done[t] = sA.inc(scalar.copy(dest, src))
                        blk.scalar(a_ev)

                        if t + 2 < NTILES:
                            blk.sync(lambda sy, t=t: load_w(sy, t + 2))

                    hd = sV.n
                    ha = sA.n

                    wstore_toks = []

                    def a_wstore(scalar):
                        scalar.wait_ge(sV.h, hd)
                        scalar.wait_ge(sA.h, ha)
                        ins = scalar.dma_start(
                            out=wq1t_hbm.rearrange("(a p) f -> p a f", p=P),
                            in_=wq1t_vv)
                        wstore_toks.append(sS[0].tok(ins))
                        ins = scalar.dma_start(
                            out=wq2t_hbm.rearrange("(a p) d -> p a d", p=P),
                            in_=wq2t_vv)
                        wstore_toks.append(sS[1].tok(ins))
                    blk.scalar(a_wstore)

            if last_phase < 2:
                return nc
            emit_ffn(nc, "p2", x_in, x1_hbm, wq1t[:], wq2t[:], None, None, None,
                     ones_row_bf, b1h, b2h_bf, ident_bf, eps_c,
                     sL, sS, sT, sV, sA, sG, st_ready=wstore_toks)

        if last_phase < 3:
            return nc
        cW = ctx.enter_context(nc.sbuf_tensor("wbig", [P, 4 * ET * R], BF16))
        wbig = cW[:]
        qT = wbig[:, 0 * ET * R:1 * ET * R]
        kT = wbig[:, 1 * ET * R:2 * ET * R]
        vN = wbig[:, 2 * ET * R:3 * ET * R]
        wqb = wbig[:, 3 * ET * R:3 * ET * R + DT * D]
        wkb = wbig[:, 3 * ET * R + DT * D:4 * ET * R]

        with ExitStack() as c3:
            SB3 = lambda nm, sh, dt: c3.enter_context(nc.sbuf_tensor(nm, sh, dt))
            wvb = SB3("wvb", [P, DT * D], BF16)
            wst3 = [SB3(f"wst3{i}", [P, 1024], F32) for i in range(2)]
            wstv = [SB3(f"wstv{i}", [P, 1024], F32) for i in range(2)]
            xrb3 = [SB3(f"xrb3{i}", [P, 2 * D], F32) for i in range(2)]
            xn3 = [SB3(f"xn3{i}", [P, 2 * D], BF16) for i in range(2)]
            xn3T = [SB3(f"xn3T{i}", [P, DT * RB], BF16) for i in range(2)]
            ssum3 = SB3("ssum3", [P, 2 * NRB], F32)
            ssq3 = SB3("ssq3", [P, 2 * NRB], F32)
            mean3 = SB3("mean3", [P, 2 * NRB], F32)
            msq3 = SB3("msq3", [P, 2 * NRB], F32)
            vp3 = SB3("vp3", [P, 2 * NRB], F32)
            rst3 = SB3("rst3", [P, 2 * NRB], F32)
            rsti3 = SB3("rsti3", [P, 2 * NRB], F32)
            sqg3 = SB3("sqg3", [P, D], F32)
            ps_qk = [c3.enter_context(nc.psum_tensor(f"psqk{i}", [P, RB], F32))
                     for i in range(4)]
            ps_v = c3.enter_context(nc.psum_tensor("psv", [P, D], F32))
            ps_x3 = c3.enter_context(nc.psum_tensor("psx3", [P, 2 * D], BF16))

            x1_ready = [(sS[0], sS[0].n), (sS[1], sS[1].n)]
            ld_q = [None] * 8
            ld_k = [None] * 8
            ld_v = [None] * 8
            castq = [0] * 8
            castk = [0] * 8
            castv = [0] * 8
            ld_x1 = [None] * NRB
            lnA3 = [0] * NRB
            sqd3 = [0] * NRB
            xnd3 = [0] * NRB
            trd3 = [0] * NRB
            xnT3d = [0] * NRB
            qk_mm = [[0] * 16 for _ in range(NRB)]
            qk_ev = [[0] * 16 for _ in range(NRB)]
            v_mm = [[0, 0] for _ in range(NRB)]
            v_ev = [[0, 0] for _ in range(NRB)]

            with nc.Block("p3") as blk:
                def load_qk(gq, n):
                    if n >= 2:
                        prev = castq[n - 2] if n - 2 < 8 else castk[n - 10]
                        gq.wait_ge((sV if n - 2 < 8 else sA).h, prev)
                    w_ap = wq_in if n < 8 else wk_in
                    ch = n % 8
                    ins = gq.dma_start(out=wst3[n % 2][:],
                                       in_=w_ap[ch * P:(ch + 1) * P, :])
                    if n < 8:
                        ld_q[ch] = sC[n % 2].tok(ins)
                    else:
                        ld_k[ch] = sC[n % 2].tok(ins)

                def load_v(sync, ch):
                    if ch >= 2:
                        sync.wait_ge(sG.h, castv[ch - 2])
                    ins = sync.dma_start(out=wstv[ch % 2][:],
                                         in_=wv_in[ch * P:(ch + 1) * P, :])
                    ld_v[ch] = sL[2 + ch % 2].tok(ins)

                def load_x1(sync, j):
                    for tok in x1_ready:
                        wdma(sync, tok)
                    if j >= 2:
                        sync.wait_ge(sV.h, lnA3[j - 2])
                        sync.wait_ge(sV.h, xnd3[j - 2])
                    src = x1_hbm[j * RB:(j + 1) * RB, :] \
                        .rearrange("(t p) d -> p t d", p=P)
                    ins = sync.dma_start(
                        out=xrb3[j % 2][:].rearrange("p (t d) -> p t d", t=2),
                        in_=src)
                    ld_x1[j] = sL[j % 2].tok(ins)

                def s_x01(sync):
                    load_x1(sync, 0)
                    load_x1(sync, 1)
                    load_v(sync, 0)
                    load_v(sync, 1)
                blk.sync(s_x01)

                def g_w01(gpsimd):
                    for tok in x1_ready:
                        wdma(gpsimd, tok)
                    load_qk(gpsimd, 0)
                    load_qk(gpsimd, 1)
                blk.gpsimd(g_w01)

                acc3_tok = [0] * NRB

                def a_acc3(scalar, j):
                    wdma(scalar, ld_x1[j])
                    acc3_tok[j] = emit_ln_acc(scalar, xrb3[j % 2][:], j, sqg3,
                                              ssum3, ssq3, sA)

                def v_comb3(vector, j):
                    lnA3[j] = emit_ln_comb(vector, j, acc3_tok[j], ssum3,
                                           ssq3, mean3, msq3, vp3, sV, sA)

                def a_sqrt3(scalar, j):
                    scalar.wait_ge(sV.h, lnA3[j])
                    ins = scalar.activation(rst3[:, 2 * j:2 * j + 2],
                                            vp3[:, 2 * j:2 * j + 2], AF.Sqrt,
                                            bias=eps_c[:, 1:2])
                    sqd3[j] = sA.inc(ins)

                def v_xn3(vector, j):
                    vector.wait_ge(sA.h, sqd3[j])
                    n = sV.inc(vector.reciprocal(rsti3[:, 2 * j:2 * j + 2],
                                                 rst3[:, 2 * j:2 * j + 2]))
                    vector.wait_ge(sV.h, n)
                    if j >= 2:
                        vector.wait_ge(sT.h, trd3[j - 2])
                    ins = None
                    for rt in range(2):
                        g = 2 * j + rt
                        ins = vector.tensor_scalar(
                            xn3[j % 2][:, rt * D:(rt + 1) * D],
                            xrb3[j % 2][:, rt * D:(rt + 1) * D],
                            mean3[:, g:g + 1], rsti3[:, g:g + 1],
                            ALU.subtract, ALU.mult)
                    xnd3[j] = sV.inc(ins)

                def t_tr3(tensor, j):
                    tensor.wait_ge(sV.h, xnd3[j])
                    if j >= 1:
                        tensor.wait_ge(sA.h, xnT3d[j - 1])
                    for rt in range(2):
                        for dt in range(DT):
                            tensor.transpose(
                                ps_x3[:, rt * D + dt * P: rt * D + (dt + 1) * P],
                                xn3[j % 2][:, rt * D + dt * P:
                                            rt * D + (dt + 1) * P],
                                ident_bf[:])
                    trd3[j] = sT.inc(tensor.drain())

                def a_xnT3(scalar, j):
                    scalar.wait_ge(sT.h, trd3[j])
                    dst = xn3T[j % 2][:].rearrange("p (dt r) -> p dt r", dt=DT) \
                        .rearrange("p dt (rt c) -> p dt rt c", rt=2)
                    src = ps_x3[:].rearrange("p (rt dt c) -> p dt rt c",
                                             rt=2, dt=DT)
                    xnT3d[j] = sA.inc(scalar.copy(dst, src))

                blk.scalar(lambda sc: a_acc3(sc, 0))
                blk.vector(lambda v: v_comb3(v, 0))
                blk.scalar(lambda sc: a_sqrt3(sc, 0))
                blk.vector(lambda v: v_xn3(v, 0))
                blk.tensor(lambda t: t_tr3(t, 0))
                blk.scalar(lambda sc: a_xnT3(sc, 0))

                for n in range(16):
                    if n < 8:
                        def v_cq(vector, n=n):
                            wdma(vector, ld_q[n])
                            castq[n] = sV.inc(vector.tensor_copy(
                                wqb[:, n * D:(n + 1) * D], wst3[n % 2][:]))
                        blk.vector(v_cq)
                    else:
                        def a_ck(scalar, n=n):
                            ch = n - 8
                            wdma(scalar, ld_k[ch])
                            castk[ch] = sA.inc(scalar.copy(
                                wkb[:, ch * D:(ch + 1) * D], wst3[n % 2][:]))
                        blk.scalar(a_ck)
                    if n + 2 < 16:
                        blk.gpsimd(lambda g, n=n: load_qk(g, n + 2))
                for ch in range(8):
                    def g_cv(gpsimd, ch=ch):
                        wdma(gpsimd, ld_v[ch])
                        castv[ch] = sG.inc(gpsimd.tensor_copy(
                            wvb[:, ch * D:(ch + 1) * D], wstv[ch % 2][:]))
                    blk.gpsimd(g_cv)
                    if ch + 2 < 8:
                        blk.sync(lambda sy, ch=ch: load_v(sy, ch + 2))

                for j in range(NRB):
                    if j + 1 < NRB:
                        blk.scalar(lambda sc, j=j: a_acc3(sc, j + 1))

                    vbase = sV.n
                    ln_incs = 8 if j + 1 < NRB else 0
                    pred_ev = [0] * 16
                    for idx in range(8):
                        pred_ev[idx] = vbase + 1 + idx
                    for idx in range(8, 16):
                        pred_ev[idx] = vbase + 8 + ln_incs + (idx - 8) + 1

                    def t_qk(tensor, j=j, pred_ev=pred_ev):
                        for which in range(2):
                            wb = (wqb, wkb)[which]
                            for et in range(ET):
                                idx = which * 8 + et
                                if idx == 0:
                                    tensor.wait_ge(sA.h, xnT3d[j])
                                    tensor.wait_ge(sV.h, castq[7])
                                if idx == 8:
                                    tensor.wait_ge(sA.h, castk[7])
                                if idx >= 4:
                                    tensor.wait_ge(sV.h, pred_ev[idx - 4])
                                elif j >= 1:
                                    tensor.wait_ge(sV.h, qk_ev[j - 1][12 + idx])
                                ins = None
                                for dt in range(DT):
                                    ins = tensor.matmul(
                                        ps_qk[idx % 4][:],
                                        wb[:, dt * D + et * P:
                                           dt * D + (et + 1) * P],
                                        xn3T[j % 2][:, dt * RB:(dt + 1) * RB],
                                        start=(dt == 0), stop=(dt == DT - 1))
                                qk_mm[j][idx] = sT.inc(ins)
                    blk.tensor(t_qk)

                    def v_ev_half(vector, j, which):
                        dstT = (qT, kT)[which]
                        for et in range(ET):
                            idx = which * 8 + et
                            vector.wait_ge(sT.h, qk_mm[j][idx])
                            qk_ev[j][idx] = sV.inc(vector.tensor_copy(
                                dstT[:, et * R + j * RB:
                                     et * R + (j + 1) * RB],
                                ps_qk[idx % 4][:]))
                    blk.vector(lambda v, j=j: v_ev_half(v, j, 0))
                    if j + 1 < NRB:
                        blk.vector(lambda v, j=j: v_comb3(v, j + 1))
                        blk.scalar(lambda sc, j=j: a_sqrt3(sc, j + 1))
                        blk.vector(lambda v, j=j: v_xn3(v, j + 1))
                    blk.vector(lambda v, j=j: v_ev_half(v, j, 1))

                    if j + 1 < NRB:
                        blk.tensor(lambda t, j=j: t_tr3(t, j + 1))
                        blk.scalar(lambda sc, j=j: a_xnT3(sc, j + 1))

                    pred_vev0 = sV.n + 1

                    def t_v(tensor, j=j, pred_vev0=pred_vev0):
                        tensor.wait_ge(sG.h, castv[7])
                        for rt in range(2):
                            if rt == 0:
                                if j >= 1:
                                    tensor.wait_ge(sV.h, v_ev[j - 1][1])
                            else:
                                tensor.wait_ge(sV.h, pred_vev0)
                            ins = None
                            for dt in range(DT):
                                for h in range(2):
                                    ins = tensor.matmul(
                                        ps_v[:, h * 512:(h + 1) * 512],
                                        xn3T[j % 2][:, dt * RB + rt * P:
                                                    dt * RB + (rt + 1) * P],
                                        wvb[:, dt * D + h * 512:
                                            dt * D + (h + 1) * 512],
                                        start=(dt == 0), stop=(dt == DT - 1))
                            v_mm[j][rt] = sT.inc(ins)
                    blk.tensor(t_v)

                    def v_vev(vector, j=j):
                        for rt in range(2):
                            vector.wait_ge(sT.h, v_mm[j][rt])
                            g = 2 * j + rt
                            v_ev[j][rt] = sV.inc(vector.tensor_copy(
                                vN[:, g * D:(g + 1) * D], ps_v[:]))
                    blk.vector(v_vev)
                    assert sV.n == pred_vev0 + 1, "p3 token prediction drift"
                    assert v_ev[j][0] == pred_vev0, "p3 vev prediction drift"

                    if j + 2 < NRB:
                        blk.sync(lambda sy, j=j: load_x1(sy, j + 2))

        if last_phase < 4:
            return nc
        wq1t5 = wbig[:, 0:DT * F]
        with ExitStack() as c4:
            x1t = [c4.enter_context(nc.sbuf_tensor(f"x1t{i}", [P, D], F32))
                   for i in range(3)]
            P_sb = [c4.enter_context(nc.sbuf_tensor(f"Psb{i}", [P, R], BF16))
                    for i in range(3)]
            PT_sb = c4.enter_context(nc.sbuf_tensor("PTsb", [P, R], BF16))
            negmax = c4.enter_context(nc.sbuf_tensor("negmax", [P, RT], F32))
            sumx = c4.enter_context(nc.sbuf_tensor("sumx", [P, RT], F32))
            rs = c4.enter_context(nc.sbuf_tensor("rs", [P, RT], F32))
            ps_s = c4.enter_context(nc.psum_tensor("ps_s", [P, R], F32))
            ps_pt = c4.enter_context(nc.psum_tensor("ps_pt", [P, R], BF16))
            ps_av = c4.enter_context(nc.psum_tensor("ps_av", [P, D], F32))

            NJT = RT
            ld_x1t = [None] * NJT
            sc_tok = [0] * NJT
            max_tok = [0] * NJT
            exp_tok = [0] * NJT
            ptt = [0] * NJT
            pte = [0] * NJT
            av_tok = [0] * NJT
            ave = [0] * NJT
            st_x2 = [None] * NJT
            exp_last_guard = [0]

            with nc.Block("p4") as blk:
                def load_x1t(sync, jt):
                    if jt >= 3:
                        wdma(sync, st_x2[jt - 3])
                    ins = sync.dma_start(out=x1t[jt % 3][:],
                                         in_=x1_hbm[jt * P:(jt + 1) * P, :])
                    ld_x1t[jt] = sL[jt % 3].tok(ins)

                def s_x01(sync):
                    load_x1t(sync, 0)
                    load_x1t(sync, 1)
                blk.sync(s_x01)

                for jt in range(NJT + 2):
                    if jt >= 2:
                        def t_pt(tensor, jt=jt):
                            jj = jt - 2
                            tensor.wait_ge(sA.h, exp_tok[jj])
                            if jj >= 1:
                                tensor.wait_ge(sV.h, pte[jj - 1])
                            for it in range(RT):
                                tensor.transpose(
                                    ps_pt[:, it * P:(it + 1) * P],
                                    P_sb[jj % 3][:, it * P:(it + 1) * P],
                                    ident_bf[:])
                            ptt[jj] = sT.inc(tensor.drain())
                        blk.tensor(t_pt)

                    if jt < NJT:
                        def t_sc(tensor, jt=jt):
                            if jt >= 1:
                                tensor.wait_ge(sA.h, exp_tok[jt - 1])
                            ins = None
                            for seg in range(4):
                                for et in range(ET):
                                    ins = tensor.matmul(
                                        ps_s[:, seg * 512:(seg + 1) * 512],
                                        kT[:, et * R + jt * P:
                                           et * R + (jt + 1) * P],
                                        qT[:, et * R + seg * 512:
                                           et * R + (seg + 1) * 512],
                                        start=(et == 0), stop=(et == ET - 1))
                            sc_tok[jt] = sT.inc(ins)
                        blk.tensor(t_sc)

                    if jt >= 2:
                        def v_pte(vector, jt=jt):
                            jj = jt - 2
                            vector.wait_ge(sT.h, ptt[jj])
                            if jj >= 1:
                                tok = av_tok[jj - 1]
                                vector.wait_ge(sT.h, tok)
                            pte[jj] = sV.inc(vector.tensor_copy(PT_sb[:],
                                                                ps_pt[:]))
                        blk.vector(v_pte)

                    if jt < NJT:
                        def v_max(vector, jt=jt):
                            vector.wait_ge(sT.h, sc_tok[jt])
                            n = sV.inc(vector.reduce_max(
                                negmax[:, jt:jt + 1], ps_s[:],
                                axis=AX.X, negate=True))
                            vector.wait_ge(sV.h, n)
                            max_tok[jt] = sV.inc(vector.tensor_scalar(
                                negmax[:, jt:jt + 1], negmax[:, jt:jt + 1],
                                1.0 / 32.0, None, ALU.mult))
                        blk.vector(v_max)

                    if jt < NJT:
                        def a_exp(scalar, jt=jt):
                            scalar.wait_ge(sV.h, max_tok[jt])
                            if jt >= 3:
                                scalar.wait_ge(sT.h, ptt[jt - 3])
                            ins = scalar.activation(
                                P_sb[jt % 3][:], ps_s[:], AF.Exp,
                                bias=negmax[:, jt:jt + 1], scale=1.0 / 32.0,
                                accum_out=sumx[:, jt:jt + 1])
                            exp_tok[jt] = sA.inc(ins)
                            if jt == NJT - 1:
                                exp_last_guard[0] = sA.inc(scalar.drain())
                        blk.scalar(a_exp)

                    if jt >= 2:
                        def t_av(tensor, jt=jt):
                            jj = jt - 2
                            tensor.wait_ge(sV.h, pte[jj])
                            if jj >= 2:
                                tensor.wait_ge(sV.h, ave[jj - 2])
                            ins = None
                            for it in range(RT):
                                for h in range(2):
                                    ins = tensor.matmul(
                                        ps_av[:, h * 512:(h + 1) * 512],
                                        PT_sb[:, it * P:(it + 1) * P],
                                        vN[:, it * D + h * 512:
                                           it * D + (h + 1) * 512],
                                        start=(it == 0), stop=(it == RT - 1))
                            av_tok[jj] = sT.inc(ins)
                        blk.tensor(t_av)

                        def v_ave(vector, jt=jt):
                            jj = jt - 2
                            guard = exp_tok[jj + 1] if jj + 1 < NJT \
                                else exp_last_guard[0]
                            vector.wait_ge(sA.h, guard)
                            n = sV.inc(vector.reciprocal(rs[:, jj:jj + 1],
                                                         sumx[:, jj:jj + 1]))
                            vector.wait_ge(sV.h, n)
                            vector.wait_ge(sT.h, av_tok[jj])
                            wdma(vector, ld_x1t[jj])
                            ave[jj] = sV.inc(vector.scalar_tensor_tensor(
                                x1t[jj % 3][:], ps_av[:], rs[:, jj:jj + 1],
                                x1t[jj % 3][:], ALU.mult, ALU.add))
                        blk.vector(v_ave)

                        def a_st(scalar, jt=jt):
                            jj = jt - 2
                            scalar.wait_ge(sV.h, ave[jj])
                            ins = scalar.dma_start(
                                out=x2_hbm[jj * P:(jj + 1) * P, :],
                                in_=x1t[jj % 3][:])
                            st_x2[jj] = (sS[0], sS[1], sC[0])[jj % 3].tok(ins)
                        blk.scalar(a_st)

                    if 2 <= jt + 1 < NJT:
                        blk.sync(lambda sy, jt=jt: load_x1t(sy, jt + 1))

                def g_w15(gpsimd):
                    gpsimd.wait_ge(sT.h, sc_tok[NJT - 1])
                    ins = gpsimd.dma_start(
                        out=wq1t5.rearrange("p (a f) -> p a f", a=DT),
                        in_=wq1t_hbm.rearrange("(a p) f -> p a f", p=P))
                    sW.inc(ins, 16)
                blk.gpsimd(g_w15)

        if last_phase < 5:
            return nc
        with ExitStack() as c5:
            wq2t5 = wbig[:, 2 * ET * R:4 * ET * R]
            x2_ready = [(sS[0], sS[0].n), (sS[1], sS[1].n), (sC[0], sC[0].n)]
            wq2_chunks = []

            def pre_body(blk):
                def g_wq2(gpsimd):
                    for tok in x2_ready:
                        wdma(gpsimd, tok)
                    src = wq2t_hbm.rearrange("(a p) d -> p a d", p=P)
                    dst = wq2t5.rearrange("p (a d) -> p a d", a=FT)
                    sems = [sC[0], sC[1], sL[2], sL[3]]
                    for cc in range(4):
                        ins = gpsimd.dma_start(
                            out=dst[:, cc * 8:(cc + 1) * 8, :],
                            in_=src[:, cc * 8:(cc + 1) * 8, :])
                        wq2_chunks.append(sems[cc].tok(ins))
                blk.gpsimd(g_wq2)

            emit_ffn(nc, "p5", x2_hbm, out_hbm, wq1t5, wq2t5,
                     (sW, sW.n), wq2_chunks, x2_ready,
                     ones_row_bf, b1h, b2h_bf, ident_bf, eps_c,
                     sL, sS, sT, sV, sA, sG,
                     final_wait=True, pre_body=pre_body)

    return nc


_NC_CACHE = {}


def kernel(L0_attn_x, L0_attn_W_query, L0_attn_W_key, L0_attn_W_value,
           L0_ffn_W1, L0_ffn_b1, L0_ffn_W2, L0_ffn_b2):
    x = np.ascontiguousarray(np.asarray(L0_attn_x, dtype=np.float32))
    wq = np.ascontiguousarray(np.asarray(L0_attn_W_query, dtype=np.float32))
    wk = np.ascontiguousarray(np.asarray(L0_attn_W_key, dtype=np.float32))
    wv = np.ascontiguousarray(np.asarray(L0_attn_W_value, dtype=np.float32))
    w1 = np.ascontiguousarray(np.asarray(L0_ffn_W1, dtype=np.float32))
    b1 = np.ascontiguousarray(np.asarray(L0_ffn_b1, dtype=np.float32))
    w2 = np.ascontiguousarray(np.asarray(L0_ffn_W2, dtype=np.float32))
    b2 = np.ascontiguousarray(np.asarray(L0_ffn_b2, dtype=np.float32))

    if "nc" not in _NC_CACHE:
        _NC_CACHE["nc"] = build_program()
    nc = _NC_CACHE["nc"]

    in_maps = [
        {"x": x[c], "wq": wq, "wk": wk, "wv": wv,
         "w1": w1, "b1": b1, "w2": w2, "b2": b2}
        for c in range(N_CORES)
    ]
    res = run_bass_kernel_spmd(nc, in_maps, core_ids=list(range(N_CORES)))
    return np.stack([np.asarray(res.results[c]["out"], dtype=np.float32)
                     for c in range(N_CORES)], axis=0)


# version 16
# speedup vs baseline: 1.0228x; 1.0228x over previous
import numpy as np
from contextlib import ExitStack

import concourse.bass as bass
import concourse.mybir as mybir
from concourse.bass_utils import run_bass_kernel_spmd

F32 = mybir.dt.float32
BF16 = mybir.dt.bfloat16
AF = mybir.ActivationFunctionType
ALU = mybir.AluOpType
AX = mybir.AxisListType

P = 128
R = 2048
D = 1024
F = 4096
DT = D // P
FT = F // P
ET = D // P
RT = R // P
RB = 256
NRB = R // RB
EPS = 1e-5
N_CORES = 8


class Sem:
    def __init__(self, nc, ctx, name):
        self.h = ctx.enter_context(nc.semaphore(name))
        self.n = 0

    def inc(self, instr, k=1):
        instr.then_inc(self.h, k)
        self.n += k
        return self.n

    def tok(self, instr, k=16):
        return (self, self.inc(instr, k))


def wdma(eng, tok):
    eng.wait_ge(tok[0].h, tok[1])


def emit_ln_acc(scalar, xsrc, i, junk, ssum, ssq, sA):
    for rt in range(2):
        g = 2 * i + rt
        xt = xsrc[:, rt * D:(rt + 1) * D]
        n = sA.inc(scalar.activation(junk[:], xt, AF.Square,
                                     accum_out=ssq[:, g:g + 1]))
        scalar.wait_ge(sA.h, n)
        n = sA.inc(scalar.activation(junk[:], xt, AF.Copy,
                                     accum_out=ssum[:, g:g + 1]))
        scalar.wait_ge(sA.h, n)
    return sA.inc(scalar.drain())


def emit_ln_comb(v, i, acc_tok, ssum, ssq, mean, msq, vp, sV, sA):
    v.wait_ge(sA.h, acc_tok)
    n = 0
    for rt in range(2):
        g = 2 * i + rt
        n = sV.inc(v.tensor_scalar(mean[:, g:g + 1], ssum[:, g:g + 1],
                                   1.0 / 1024.0, None, ALU.mult))
        v.wait_ge(sV.h, n)
        n = sV.inc(v.scalar_tensor_tensor(msq[:, g:g + 1], mean[:, g:g + 1],
                                          1.0, mean[:, g:g + 1],
                                          ALU.mult, ALU.mult))
        v.wait_ge(sV.h, n)
        n = sV.inc(v.scalar_tensor_tensor(vp[:, g:g + 1], ssq[:, g:g + 1],
                                          1.0 / 1024.0, msq[:, g:g + 1],
                                          ALU.mult, ALU.subtract))
        v.wait_ge(sV.h, n)
    return n


def emit_ffn(nc, name, src_hbm, dst_hbm, wq1t_v, wq2t_v, wq1_ready, wq2_chunks,
             src_ready, ones_row_bf, b1h, b2h_bf, ident_bf, eps_c,
             sL, sS, sT, sV, sA, sG, final_wait=False, pre_body=None,
             st_ready=None):
    with ExitStack() as c:
        SB = lambda nm, sh, dt: c.enter_context(
            nc.sbuf_tensor(f"{name}_{nm}", sh, dt))
        xrb = [SB(f"x{i}", [P, 2 * D], F32) for i in range(2)]
        xn = [SB(f"xn{i}", [P, 2 * D], BF16) for i in range(2)]
        xnT = [SB(f"xnT{i}", [P, DT * RB], BF16) for i in range(2)]
        hT = SB("hT", [P, FT * RB], BF16)
        ssum = SB("ssum", [P, 2 * NRB], F32)
        ssq = SB("ssq", [P, 2 * NRB], F32)
        mean = SB("mean", [P, 2 * NRB], F32)
        msq = SB("msq", [P, 2 * NRB], F32)
        vp = SB("vp", [P, 2 * NRB], F32)
        rst = SB("rst", [P, 2 * NRB], F32)
        rsti = SB("rsti", [P, 2 * NRB], F32)
        sqg = SB("sqg", [P, D], F32)
        ps_h = [c.enter_context(nc.psum_tensor(f"{name}_psh{i}", [P, RB], F32))
                for i in range(4)]
        ps_oA = c.enter_context(nc.psum_tensor(f"{name}_psoA", [P, 512], F32))
        ps_oB = c.enter_context(nc.psum_tensor(f"{name}_psoB", [P, 512], F32))
        ps_x = c.enter_context(nc.psum_tensor(f"{name}_psx", [P, 2 * D], BF16))

        ld_x = [None] * NRB
        lnA = [0] * NRB
        sqd = [0] * NRB
        xnd = [0] * NRB
        trd = [0] * NRB
        xnTd = [0] * NRB
        h_mm = [0] * (NRB * FT)
        h_ev = [0] * (NRB * FT)
        o_mmA = [[0, 0] for _ in range(NRB)]
        o_mmB = [[0, 0] for _ in range(NRB)]
        st_done = [None] * NRB
        last_res = [0, 0]

        with nc.Block(name) as blk:
            if pre_body is not None:
                pre_body(blk)

            def load_x(sync, i):
                for tok in (src_ready or []):
                    wdma(sync, tok)
                if i >= 2:
                    wdma(sync, st_done[i - 2])
                src = src_hbm[i * RB:(i + 1) * RB, :] \
                    .rearrange("(t p) d -> p t d", p=P)
                ins = sync.dma_start(
                    out=xrb[i % 2][:].rearrange("p (t d) -> p t d", t=2),
                    in_=src)
                ld_x[i] = sL[i % 2].tok(ins)

            def s_x01(sync):
                load_x(sync, 0)
                load_x(sync, 1)
            blk.sync(s_x01)

            acc_tok = [0] * NRB

            def a_acc(scalar, i):
                wdma(scalar, ld_x[i])
                acc_tok[i] = emit_ln_acc(scalar, xrb[i % 2][:], i, sqg,
                                         ssum, ssq, sA)

            def v_comb(vector, i):
                lnA[i] = emit_ln_comb(vector, i, acc_tok[i], ssum, ssq,
                                      mean, msq, vp, sV, sA)

            def a_sqrt(scalar, i):
                scalar.wait_ge(sV.h, lnA[i])
                ins = scalar.activation(rst[:, 2 * i:2 * i + 2],
                                        vp[:, 2 * i:2 * i + 2], AF.Sqrt,
                                        bias=eps_c[:, 0:1], scale=4.0)
                sqd[i] = sA.inc(ins)

            def v_xn(vector, i):
                vector.wait_ge(sA.h, sqd[i])
                n = sV.inc(vector.reciprocal(rsti[:, 2 * i:2 * i + 2],
                                             rst[:, 2 * i:2 * i + 2]))
                vector.wait_ge(sV.h, n)
                if i >= 2:
                    vector.wait_ge(sT.h, trd[i - 2])
                ins = None
                for rt in range(2):
                    g = 2 * i + rt
                    ins = vector.tensor_scalar(
                        xn[i % 2][:, rt * D:(rt + 1) * D],
                        xrb[i % 2][:, rt * D:(rt + 1) * D],
                        mean[:, g:g + 1], rsti[:, g:g + 1],
                        ALU.subtract, ALU.mult)
                xnd[i] = sV.inc(ins)

            def t_tr(tensor, i):
                tensor.wait_ge(sV.h, xnd[i])
                if i >= 1:
                    tensor.wait_ge(sA.h, xnTd[i - 1])
                for rt in range(2):
                    for dt in range(DT):
                        tensor.transpose(
                            ps_x[:, rt * D + dt * P: rt * D + (dt + 1) * P],
                            xn[i % 2][:, rt * D + dt * P: rt * D + (dt + 1) * P],
                            ident_bf[:])
                trd[i] = sT.inc(tensor.drain())

            def a_xnT(scalar, i):
                scalar.wait_ge(sT.h, trd[i])
                dst = xnT[i % 2][:].rearrange("p (dt r) -> p dt r", dt=DT) \
                    .rearrange("p dt (rt c) -> p dt rt c", rt=2)
                src = ps_x[:].rearrange("p (rt dt c) -> p dt rt c", rt=2, dt=DT)
                xnTd[i] = sA.inc(scalar.copy(dst, src))

            blk.scalar(lambda sc: a_acc(sc, 0))
            blk.vector(lambda v: v_comb(v, 0))
            blk.scalar(lambda sc: a_sqrt(sc, 0))
            blk.vector(lambda v: v_xn(v, 0))
            blk.tensor(lambda t: t_tr(t, 0))
            blk.scalar(lambda sc: a_xnT(sc, 0))

            for i in range(NRB):
                if i + 1 < NRB:
                    blk.scalar(lambda sc, i=i: a_acc(sc, i + 1))

                vb = sV.n
                ln_incs = (6, 2) if i + 1 < NRB else (0, 0)
                pred_hev = [0] * FT
                for fc in range(16):
                    pred_hev[fc] = vb + 1 + fc
                for fc in range(16, 24):
                    pred_hev[fc] = vb + 16 + ln_incs[0] + (fc - 16) + 1
                for fc in range(24, FT):
                    pred_hev[fc] = vb + 24 + sum(ln_incs) + (fc - 24) + 1

                def t_mm1(tensor, i=i, pred_hev=pred_hev):
                    for fc in range(FT):
                        g = i * FT + fc
                        if fc == 0:
                            tensor.wait_ge(sA.h, xnTd[i])
                            if i == 0 and wq1_ready is not None:
                                wdma(tensor, wq1_ready)
                        if fc >= 4:
                            tensor.wait_ge(sV.h, pred_hev[fc - 4])
                        elif g >= 4:
                            tensor.wait_ge(sV.h, h_ev[g - 4])
                        ins = None
                        for dt in range(DT):
                            ins = tensor.matmul(
                                ps_h[fc % 4][:],
                                wq1t_v[:, dt * F + fc * P: dt * F + (fc + 1) * P],
                                xnT[i % 2][:, dt * RB:(dt + 1) * RB],
                                start=(dt == 0), stop=(dt == DT - 1))
                        h_mm[g] = sT.inc(ins)
                blk.tensor(t_mm1)

                def v_ev_range(vector, i, lo, hi):
                    for fc in range(lo, hi):
                        g = i * FT + fc
                        vector.wait_ge(sT.h, h_mm[g])
                        ins = vector.tensor_scalar(
                            hT[:, fc * RB:(fc + 1) * RB],
                            ps_h[fc % 4][:], b1h[:, fc:fc + 1], 0.0,
                            ALU.add, ALU.max)
                        h_ev[g] = sV.inc(ins)
                blk.vector(lambda v, i=i: v_ev_range(v, i, 0, 16))
                if i + 1 < NRB:
                    blk.vector(lambda v, i=i: v_comb(v, i + 1))
                    blk.scalar(lambda sc, i=i: a_sqrt(sc, i + 1))
                blk.vector(lambda v, i=i: v_ev_range(v, i, 16, 24))
                if i + 1 < NRB:
                    blk.vector(lambda v, i=i: v_xn(v, i + 1))
                blk.vector(lambda v, i=i: v_ev_range(v, i, 24, FT))
                assert sV.n == pred_hev[FT - 1], "mm1 evac prediction drift"

                if i + 1 < NRB:
                    blk.tensor(lambda t, i=i: t_tr(t, i + 1))
                    blk.scalar(lambda sc, i=i: a_xnT(sc, i + 1))

                def t_mm2(tensor, i=i):
                    v0 = sV.n
                    pred = [v0 + 1, v0 + 2, v0 + 3, v0 + 4]
                    tensor.wait_ge(sV.h, h_ev[i * FT + FT - 1])
                    for rt in range(2):
                        if rt == 0:
                            if last_res[0]:
                                tensor.wait_ge(sV.h, last_res[0])
                            if last_res[1]:
                                tensor.wait_ge(sV.h, last_res[1])
                        else:
                            tensor.wait_ge(sV.h, pred[0])
                            tensor.wait_ge(sV.h, pred[1])
                        for fc in range(FT):
                            if wq2_chunks is not None and i == 0 and rt == 0 \
                                    and fc % 8 == 0:
                                wdma(tensor, wq2_chunks[fc // 8])
                            lhs = hT[:, fc * RB + rt * P: fc * RB + (rt + 1) * P]
                            tensor.matmul(
                                ps_oA[:], lhs,
                                wq2t_v[:, fc * D: fc * D + 512],
                                start=(fc == 0), stop=False)
                            tensor.matmul(
                                ps_oB[:], lhs,
                                wq2t_v[:, fc * D + 512: fc * D + 1024],
                                start=(fc == 0), stop=False)
                        ins = tensor.matmul(ps_oA[:], ones_row_bf[:],
                                            b2h_bf[0:1, 0:512],
                                            start=False, stop=True)
                        o_mmA[i][rt] = sT.inc(ins)
                        ins = tensor.matmul(ps_oB[:], ones_row_bf[:],
                                            b2h_bf[0:1, 512:1024],
                                            start=False, stop=True)
                        o_mmB[i][rt] = sT.inc(ins)
                    last_res[0] = pred[2]
                    last_res[1] = pred[3]
                blk.tensor(t_mm2)

                def v_res(vector, i=i):
                    for rt in range(2):
                        vector.wait_ge(sT.h, o_mmA[i][rt])
                        base = rt * D
                        sV.inc(vector.tensor_tensor(
                            xrb[i % 2][:, base:base + 512],
                            xrb[i % 2][:, base:base + 512], ps_oA[:], ALU.add))
                        vector.wait_ge(sT.h, o_mmB[i][rt])
                        sV.inc(vector.tensor_tensor(
                            xrb[i % 2][:, base + 512:base + 1024],
                            xrb[i % 2][:, base + 512:base + 1024], ps_oB[:],
                            ALU.add))
                blk.vector(v_res)
                assert sV.n == last_res[1], "res token prediction drifted"

                def a_store(scalar, i=i):
                    scalar.wait_ge(sV.h, last_res[1])
                    if i < 2 and st_ready is not None:
                        wdma(scalar, st_ready[i])
                    dst = dst_hbm[i * RB:(i + 1) * RB, :] \
                        .rearrange("(t p) d -> p t d", p=P)
                    ins = scalar.dma_start(
                        out=dst,
                        in_=xrb[i % 2][:].rearrange("p (t d) -> p t d", t=2))
                    st_done[i] = sS[i % 2].tok(ins)
                blk.scalar(a_store)

                if i + 2 < NRB:
                    blk.sync(lambda sy, i=i: load_x(sy, i + 2))

            if final_wait:
                fa, fb = sS[0].n, sS[1].n

                def a_final(scalar):
                    scalar.wait_ge(sS[0].h, fa)
                    scalar.wait_ge(sS[1].h, fb)
                blk.scalar(a_final)


def build_program(last_phase=5):
    nc = bass.Bass("TRN2", target_bir_lowering=False)

    x_in = nc.dram_tensor("x", [R, D], F32, kind="ExternalInput").ap()
    wq_in = nc.dram_tensor("wq", [D, D], F32, kind="ExternalInput").ap()
    wk_in = nc.dram_tensor("wk", [D, D], F32, kind="ExternalInput").ap()
    wv_in = nc.dram_tensor("wv", [D, D], F32, kind="ExternalInput").ap()
    w1_in = nc.dram_tensor("w1", [F, D], F32, kind="ExternalInput").ap()
    b1_in = nc.dram_tensor("b1", [F], F32, kind="ExternalInput").ap()
    w2_in = nc.dram_tensor("w2", [D, F], F32, kind="ExternalInput").ap()
    b2_in = nc.dram_tensor("b2", [D], F32, kind="ExternalInput").ap()
    out_hbm = nc.dram_tensor("out", [R, D], F32, kind="ExternalOutput").ap()

    wq1t_hbm = nc.dram_tensor("wq1t_s", [D, F], BF16, kind="Internal").ap()
    wq2t_hbm = nc.dram_tensor("wq2t_s", [F, D], BF16, kind="Internal").ap()
    x1_hbm = nc.dram_tensor("x1_s", [R, D], F32, kind="Internal").ap()
    x2_hbm = nc.dram_tensor("x2_s", [R, D], F32, kind="Internal").ap()

    with ExitStack() as ctx:
        SB = lambda name, shape, dt: ctx.enter_context(
            nc.sbuf_tensor(name, shape, dt))

        ones_col = SB("ones_col", [P, 1], F32)
        ones_row = SB("ones_row", [1, P], F32)
        ones_row_bf = SB("ones_row_bf", [1, P], BF16)
        ident_bf = SB("ident_bf", [P, P], BF16)
        nident_bf = SB("nident_bf", [P, P], BF16)
        ident_f32 = SB("ident_f32", [P, P], F32)
        b1h = SB("b1h", [P, FT], F32)
        b2h_bf = SB("b2h_bf", [1, D], BF16)
        s12b = SB("s12b", [P, 2], F32)
        eps_c = SB("eps_c", [P, 2], F32)

        sL = [Sem(nc, ctx, f"sL{i}") for i in range(4)]
        sS = [Sem(nc, ctx, f"sS{i}") for i in range(2)]
        sC = [Sem(nc, ctx, f"sC{i}") for i in range(2)]
        sT = Sem(nc, ctx, "sT")
        sV = Sem(nc, ctx, "sV")
        sA = Sem(nc, ctx, "sA")
        sG = Sem(nc, ctx, "sG")
        sW = Sem(nc, ctx, "sW")
        sS2 = Sem(nc, ctx, "sS2")

        with ExitStack() as c0:
            gstage = [c0.enter_context(nc.sbuf_tensor(f"gst{i}", [P, 8192], F32))
                      for i in range(2)]
            gacc = c0.enter_context(nc.sbuf_tensor("gacc", [P, 8], F32))
            gtot = c0.enter_context(nc.sbuf_tensor("gtot", [P, 2], F32))
            sinv = c0.enter_context(nc.sbuf_tensor("sinv", [1, 2], F32))
            b1_t32 = c0.enter_context(nc.sbuf_tensor("b1_t32", [32, P], F32))
            b2raw = c0.enter_context(nc.sbuf_tensor("b2raw", [1, D], F32))
            ps0 = c0.enter_context(nc.psum_tensor("ps0", [P, 2], F32))
            ps0b = c0.enter_context(nc.psum_tensor("ps0b", [P, 32], F32))

            w1_c = w1_in.rearrange("(c t p) d -> c p t d", p=P, t=8)
            w2_c = w2_in.rearrange("(c t p) f -> c p t f", p=P, t=2)

            ld_chunk = [None] * 8
            ld_b = [None, None]
            red_done = [0] * 8

            with nc.Block("p0") as blk:
                def load_chunk(sync, k):
                    if k < 4:
                        src = w1_c[k]
                        dst = gstage[k % 2][:].rearrange("p (t d) -> p t d", t=8)
                    else:
                        src = w2_c[k - 4]
                        dst = gstage[k % 2][:].rearrange("p (t f) -> p t f", t=2)
                    ins = sync.dma_start(out=dst, in_=src)
                    ld_chunk[k] = sL[k % 2].tok(ins)

                def s_first(sync):
                    load_chunk(sync, 0)
                    load_chunk(sync, 1)
                    ins = sync.dma_start(
                        out=b1_t32[:], in_=b1_in.rearrange("(c p) -> c p", p=P))
                    ld_b[0] = sL[2].tok(ins)
                    ins = sync.dma_start(out=b2raw[:], in_=b2_in[None, :])
                    ld_b[1] = sL[3].tok(ins)
                blk.sync(s_first)

                def gp_body(gpsimd):
                    gpsimd.memset(ones_col[:], 1.0)
                    gpsimd.memset(ones_row[:], 1.0)
                    gpsimd.memset(ones_row_bf[:], 1.0)
                    gpsimd.memset(eps_c[:, 0:1], 4.0 * EPS)
                    gpsimd.memset(eps_c[:, 1:2], EPS)
                    gpsimd.memset(ident_bf[:], 0.0)
                    gpsimd.memset(nident_bf[:], 0.0)
                    n = sG.inc(gpsimd.memset(ident_f32[:], 0.0))
                    gpsimd.wait_ge(sG.h, n)
                    gpsimd.affine_select(
                        out=ident_bf[:], in_=ident_bf[:],
                        compare_op=ALU.not_equal, fill=1.0, base=0,
                        pattern=[[-1, P]], channel_multiplier=1)
                    gpsimd.affine_select(
                        out=nident_bf[:], in_=nident_bf[:],
                        compare_op=ALU.not_equal, fill=-1.0, base=0,
                        pattern=[[-1, P]], channel_multiplier=1)
                    gpsimd.affine_select(
                        out=ident_f32[:], in_=ident_f32[:],
                        compare_op=ALU.not_equal, fill=1.0, base=0,
                        pattern=[[-1, P]], channel_multiplier=1)
                    sG.inc(gpsimd.drain())
                blk.gpsimd(gp_body)
                g_ready = sG.n

                for k in range(8):
                    def v_red(vector, k=k):
                        wdma(vector, ld_chunk[k])
                        ins = vector.reduce_sum(gacc[:, k:k + 1],
                                                gstage[k % 2][:], axis=AX.X,
                                                apply_absolute_value=True)
                        red_done[k] = sV.inc(ins)
                    blk.vector(v_red)
                    if k + 2 < 8:
                        def s_pref(sync, k=k):
                            sync.wait_ge(sV.h, red_done[k])
                            load_chunk(sync, k + 2)
                        blk.sync(s_pref)

                def v_tot(vector):
                    vector.wait_ge(sV.h, red_done[7])
                    vector.reduce_sum(gtot[:, 0:1], gacc[:, 0:4], axis=AX.X)
                    sV.inc(vector.reduce_sum(gtot[:, 1:2], gacc[:, 4:8],
                                             axis=AX.X))
                blk.vector(v_tot)
                v_tot_n = sV.n

                def t_sum(tensor):
                    tensor.wait_ge(sV.h, v_tot_n)
                    tensor.wait_ge(sG.h, g_ready)
                    ins = tensor.matmul(ps0[0:1, 0:2], ones_col[:], gtot[:],
                                        start=True, stop=True)
                    sT.inc(ins)
                blk.tensor(t_sum)
                t_sum_n = sT.n

                def v_sinv(vector):
                    vector.wait_ge(sT.h, t_sum_n)
                    n = sV.inc(vector.tensor_copy(sinv[:], ps0[0:1, 0:2]))
                    vector.wait_ge(sV.h, n)
                    n = sV.inc(vector.tensor_scalar(
                        sinv[:], sinv[:], 1.0 / 4194304.0, EPS,
                        ALU.mult, ALU.add))
                    vector.wait_ge(sV.h, n)
                    sV.inc(vector.reciprocal(sinv[:], sinv[:]))
                blk.vector(v_sinv)
                v_sinv_n = sV.n

                def t_bc(tensor):
                    tensor.wait_ge(sV.h, v_sinv_n)
                    tensor.matmul(ps0[:, 0:2], ones_row[:], sinv[:],
                                  start=True, stop=True)
                    wdma(tensor, ld_b[0])
                    tensor.transpose(ps0b[:, 0:32], b1_t32[:],
                                     ident_f32[0:32, 0:32])
                    sT.inc(tensor.drain())
                blk.tensor(t_bc)
                t_bc_n = sT.n

                def v_fin(vector):
                    vector.wait_ge(sT.h, t_bc_n)
                    vector.tensor_copy(s12b[:], ps0[:, 0:2])
                    vector.tensor_scalar(b1h[:], ps0b[:, 0:32], 0.5, None,
                                         ALU.mult)
                    wdma(vector, ld_b[1])
                    sV.inc(vector.tensor_scalar(b2h_bf[:], b2raw[:], 0.5, None,
                                                ALU.mult))
                blk.vector(v_fin)

        if last_phase < 1:
            return nc
        with ExitStack() as c12:
            wq1t = c12.enter_context(nc.sbuf_tensor("wq1t", [P, DT * F], BF16))
            wq2t = c12.enter_context(nc.sbuf_tensor("wq2t", [P, FT * D], BF16))
            wq1t_vv = wq1t[:].rearrange("p (dt f) -> p dt f", dt=DT)
            wq2t_vv = wq2t[:].rearrange("p (ft d) -> p ft d", ft=FT)

            with ExitStack() as c1:
                wstage = [c1.enter_context(nc.sbuf_tensor(f"wst{i}", [P, 1024], F32))
                          for i in range(2)]
                a_sb = [c1.enter_context(nc.sbuf_tensor(f"qa{i}", [P, 1024], BF16))
                        for i in range(2)]
                b_sb = [c1.enter_context(nc.sbuf_tensor(f"qb{i}", [P, 1024], BF16))
                        for i in range(2)]
                pswf = [c1.enter_context(nc.psum_tensor(f"pswf{i}", [P, 1024], F32))
                        for i in range(2)]

                NTILES = 64
                ld_w = [None] * NTILES
                ab_done = [0] * NTILES
                tr_done = [0] * NTILES
                ev_done = [0] * NTILES

                def w_src(t):
                    if t < 32:
                        return w1_in[t * P:(t + 1) * P, :]
                    u = t - 32
                    dt, fs = u // 4, u % 4
                    return w2_in[dt * P:(dt + 1) * P, fs * 1024:(fs + 1) * 1024]

                with nc.Block("p1") as blk:
                    def load_w(sync, t):
                        if t >= 2:
                            sync.wait_ge(sV.h, ab_done[t - 2])
                        ins = sync.dma_start(out=wstage[t % 2][:], in_=w_src(t))
                        ld_w[t] = sL[t % 2].tok(ins)

                    def s_first(sync):
                        load_w(sync, 0)
                        load_w(sync, 1)
                    blk.sync(s_first)

                    for t in range(NTILES):
                        sc = s12b[:, 0:1] if t < 32 else s12b[:, 1:2]

                        def v_ab(vector, t=t, sc=sc):
                            wdma(vector, ld_w[t])
                            if t >= 2:
                                vector.wait_ge(sT.h, tr_done[t - 2])
                            vector.tensor_scalar(
                                a_sb[t % 2][:], wstage[t % 2][:], sc, 0.5,
                                ALU.mult, ALU.is_ge)
                            ab_done[t] = sV.inc(vector.tensor_scalar(
                                b_sb[t % 2][:], wstage[t % 2][:], sc, -0.5,
                                ALU.mult, ALU.is_le))
                        blk.vector(v_ab)

                        def t_q(tensor, t=t):
                            tensor.wait_ge(sV.h, ab_done[t])
                            if t >= 2:
                                tensor.wait_ge(sA.h, ev_done[t - 2])
                            ins = None
                            for j in range(8):
                                sl = pswf[t % 2][:, j * P:(j + 1) * P]
                                tensor.matmul(sl, a_sb[t % 2][:, j * P:(j + 1) * P],
                                              ident_bf[:], start=True, stop=False)
                                ins = tensor.matmul(
                                    sl, b_sb[t % 2][:, j * P:(j + 1) * P],
                                    nident_bf[:], start=False, stop=True)
                            tr_done[t] = sT.inc(ins)
                        blk.tensor(t_q)

                        if t < 32:
                            dest = wq1t_vv[:, :, t * P:(t + 1) * P]
                        else:
                            u = t - 32
                            dt, fs = u // 4, u % 4
                            dest = wq2t_vv[:, fs * 8:(fs + 1) * 8,
                                           dt * P:(dt + 1) * P]
                        src = pswf[t % 2][:].rearrange("p (j c) -> p j c", j=8)

                        def a_ev(scalar, t=t, dest=dest, src=src):
                            scalar.wait_ge(sT.h, tr_done[t])
                            ev_done[t] = sA.inc(scalar.copy(dest, src))
                        blk.scalar(a_ev)

                        if t + 2 < NTILES:
                            blk.sync(lambda sy, t=t: load_w(sy, t + 2))

                    hd = sV.n
                    ha = sA.n

                    wstore_toks = []

                    def a_wstore(scalar):
                        scalar.wait_ge(sV.h, hd)
                        scalar.wait_ge(sA.h, ha)
                        ins = scalar.dma_start(
                            out=wq1t_hbm.rearrange("(a p) f -> p a f", p=P),
                            in_=wq1t_vv)
                        wstore_toks.append(sS[0].tok(ins))
                        ins = scalar.dma_start(
                            out=wq2t_hbm.rearrange("(a p) d -> p a d", p=P),
                            in_=wq2t_vv)
                        wstore_toks.append(sS[1].tok(ins))
                    blk.scalar(a_wstore)

            if last_phase < 2:
                return nc
            emit_ffn(nc, "p2", x_in, x1_hbm, wq1t[:], wq2t[:], None, None, None,
                     ones_row_bf, b1h, b2h_bf, ident_bf, eps_c,
                     sL, sS, sT, sV, sA, sG, st_ready=wstore_toks)

        if last_phase < 3:
            return nc
        cW = ctx.enter_context(nc.sbuf_tensor("wbig", [P, 4 * ET * R], BF16))
        wbig = cW[:]
        qT = wbig[:, 0 * ET * R:1 * ET * R]
        kT = wbig[:, 1 * ET * R:2 * ET * R]
        vN = wbig[:, 2 * ET * R:3 * ET * R]
        wqb = wbig[:, 3 * ET * R:3 * ET * R + DT * D]
        wkb = wbig[:, 3 * ET * R + DT * D:4 * ET * R]

        with ExitStack() as c3:
            SB3 = lambda nm, sh, dt: c3.enter_context(nc.sbuf_tensor(nm, sh, dt))
            wvb = SB3("wvb", [P, DT * D], BF16)
            wst3 = [SB3(f"wst3{i}", [P, 1024], F32) for i in range(2)]
            wstv = [SB3(f"wstv{i}", [P, 1024], F32) for i in range(2)]
            xrb3 = [SB3(f"xrb3{i}", [P, 2 * D], F32) for i in range(2)]
            xn3 = [SB3(f"xn3{i}", [P, 2 * D], BF16) for i in range(2)]
            xn3T = [SB3(f"xn3T{i}", [P, DT * RB], BF16) for i in range(2)]
            ssum3 = SB3("ssum3", [P, 2 * NRB], F32)
            ssq3 = SB3("ssq3", [P, 2 * NRB], F32)
            mean3 = SB3("mean3", [P, 2 * NRB], F32)
            msq3 = SB3("msq3", [P, 2 * NRB], F32)
            vp3 = SB3("vp3", [P, 2 * NRB], F32)
            rst3 = SB3("rst3", [P, 2 * NRB], F32)
            rsti3 = SB3("rsti3", [P, 2 * NRB], F32)
            sqg3 = SB3("sqg3", [P, D], F32)
            ps_qk = [c3.enter_context(nc.psum_tensor(f"psqk{i}", [P, RB], F32))
                     for i in range(4)]
            ps_v = c3.enter_context(nc.psum_tensor("psv", [P, D], F32))
            ps_x3 = c3.enter_context(nc.psum_tensor("psx3", [P, 2 * D], BF16))

            x1_ready = [(sS[0], sS[0].n), (sS[1], sS[1].n)]
            ld_q = [None] * 8
            ld_k = [None] * 8
            ld_v = [None] * 8
            castq = [0] * 8
            castk = [0] * 8
            castv = [0] * 8
            ld_x1 = [None] * NRB
            lnA3 = [0] * NRB
            sqd3 = [0] * NRB
            xnd3 = [0] * NRB
            trd3 = [0] * NRB
            xnT3d = [0] * NRB
            qk_mm = [[0] * 16 for _ in range(NRB)]
            qk_ev = [[0] * 16 for _ in range(NRB)]
            v_mm = [[0, 0] for _ in range(NRB)]
            v_ev = [[0, 0] for _ in range(NRB)]

            with nc.Block("p3") as blk:
                def load_qk(gq, n):
                    if n >= 2:
                        prev = castq[n - 2] if n - 2 < 8 else castk[n - 10]
                        gq.wait_ge((sV if n - 2 < 8 else sA).h, prev)
                    w_ap = wq_in if n < 8 else wk_in
                    ch = n % 8
                    ins = gq.dma_start(out=wst3[n % 2][:],
                                       in_=w_ap[ch * P:(ch + 1) * P, :])
                    if n < 8:
                        ld_q[ch] = sC[n % 2].tok(ins)
                    else:
                        ld_k[ch] = sC[n % 2].tok(ins)

                def load_v(sync, ch):
                    if ch >= 2:
                        sync.wait_ge(sG.h, castv[ch - 2])
                    ins = sync.dma_start(out=wstv[ch % 2][:],
                                         in_=wv_in[ch * P:(ch + 1) * P, :])
                    ld_v[ch] = sL[2 + ch % 2].tok(ins)

                def load_x1(sync, j):
                    for tok in x1_ready:
                        wdma(sync, tok)
                    if j >= 2:
                        sync.wait_ge(sV.h, lnA3[j - 2])
                        sync.wait_ge(sV.h, xnd3[j - 2])
                    src = x1_hbm[j * RB:(j + 1) * RB, :] \
                        .rearrange("(t p) d -> p t d", p=P)
                    ins = sync.dma_start(
                        out=xrb3[j % 2][:].rearrange("p (t d) -> p t d", t=2),
                        in_=src)
                    ld_x1[j] = sL[j % 2].tok(ins)

                def s_x01(sync):
                    load_x1(sync, 0)
                    load_x1(sync, 1)
                    load_v(sync, 0)
                    load_v(sync, 1)
                blk.sync(s_x01)

                def g_w01(gpsimd):
                    for tok in x1_ready:
                        wdma(gpsimd, tok)
                    load_qk(gpsimd, 0)
                    load_qk(gpsimd, 1)
                blk.gpsimd(g_w01)

                acc3_tok = [0] * NRB

                def a_acc3(scalar, j):
                    wdma(scalar, ld_x1[j])
                    acc3_tok[j] = emit_ln_acc(scalar, xrb3[j % 2][:], j, sqg3,
                                              ssum3, ssq3, sA)

                def v_comb3(vector, j):
                    lnA3[j] = emit_ln_comb(vector, j, acc3_tok[j], ssum3,
                                           ssq3, mean3, msq3, vp3, sV, sA)

                def a_sqrt3(scalar, j):
                    scalar.wait_ge(sV.h, lnA3[j])
                    ins = scalar.activation(rst3[:, 2 * j:2 * j + 2],
                                            vp3[:, 2 * j:2 * j + 2], AF.Sqrt,
                                            bias=eps_c[:, 1:2])
                    sqd3[j] = sA.inc(ins)

                def v_xn3(vector, j):
                    vector.wait_ge(sA.h, sqd3[j])
                    n = sV.inc(vector.reciprocal(rsti3[:, 2 * j:2 * j + 2],
                                                 rst3[:, 2 * j:2 * j + 2]))
                    vector.wait_ge(sV.h, n)
                    if j >= 2:
                        vector.wait_ge(sT.h, trd3[j - 2])
                    ins = None
                    for rt in range(2):
                        g = 2 * j + rt
                        ins = vector.tensor_scalar(
                            xn3[j % 2][:, rt * D:(rt + 1) * D],
                            xrb3[j % 2][:, rt * D:(rt + 1) * D],
                            mean3[:, g:g + 1], rsti3[:, g:g + 1],
                            ALU.subtract, ALU.mult)
                    xnd3[j] = sV.inc(ins)

                def t_tr3(tensor, j):
                    tensor.wait_ge(sV.h, xnd3[j])
                    if j >= 1:
                        tensor.wait_ge(sA.h, xnT3d[j - 1])
                    for rt in range(2):
                        for dt in range(DT):
                            tensor.transpose(
                                ps_x3[:, rt * D + dt * P: rt * D + (dt + 1) * P],
                                xn3[j % 2][:, rt * D + dt * P:
                                            rt * D + (dt + 1) * P],
                                ident_bf[:])
                    trd3[j] = sT.inc(tensor.drain())

                def a_xnT3(scalar, j):
                    scalar.wait_ge(sT.h, trd3[j])
                    dst = xn3T[j % 2][:].rearrange("p (dt r) -> p dt r", dt=DT) \
                        .rearrange("p dt (rt c) -> p dt rt c", rt=2)
                    src = ps_x3[:].rearrange("p (rt dt c) -> p dt rt c",
                                             rt=2, dt=DT)
                    xnT3d[j] = sA.inc(scalar.copy(dst, src))

                blk.scalar(lambda sc: a_acc3(sc, 0))
                blk.vector(lambda v: v_comb3(v, 0))
                blk.scalar(lambda sc: a_sqrt3(sc, 0))
                blk.vector(lambda v: v_xn3(v, 0))
                blk.tensor(lambda t: t_tr3(t, 0))
                blk.scalar(lambda sc: a_xnT3(sc, 0))

                for n in range(16):
                    if n < 8:
                        def v_cq(vector, n=n):
                            wdma(vector, ld_q[n])
                            castq[n] = sV.inc(vector.tensor_copy(
                                wqb[:, n * D:(n + 1) * D], wst3[n % 2][:]))
                        blk.vector(v_cq)
                    else:
                        def a_ck(scalar, n=n):
                            ch = n - 8
                            wdma(scalar, ld_k[ch])
                            castk[ch] = sA.inc(scalar.copy(
                                wkb[:, ch * D:(ch + 1) * D], wst3[n % 2][:]))
                        blk.scalar(a_ck)
                    if n + 2 < 16:
                        blk.gpsimd(lambda g, n=n: load_qk(g, n + 2))
                for ch in range(8):
                    def g_cv(gpsimd, ch=ch):
                        wdma(gpsimd, ld_v[ch])
                        castv[ch] = sG.inc(gpsimd.tensor_copy(
                            wvb[:, ch * D:(ch + 1) * D], wstv[ch % 2][:]))
                    blk.gpsimd(g_cv)
                    if ch + 2 < 8:
                        blk.sync(lambda sy, ch=ch: load_v(sy, ch + 2))

                for j in range(NRB):
                    if j + 1 < NRB:
                        blk.scalar(lambda sc, j=j: a_acc3(sc, j + 1))

                    vbase = sV.n
                    ln_incs = 8 if j + 1 < NRB else 0
                    pred_ev = [0] * 16
                    for idx in range(8):
                        pred_ev[idx] = vbase + 1 + idx
                    for idx in range(8, 16):
                        pred_ev[idx] = vbase + 8 + ln_incs + (idx - 8) + 1

                    def t_qk(tensor, j=j, pred_ev=pred_ev):
                        for which in range(2):
                            wb = (wqb, wkb)[which]
                            for et in range(ET):
                                idx = which * 8 + et
                                if idx == 0:
                                    tensor.wait_ge(sA.h, xnT3d[j])
                                    tensor.wait_ge(sV.h, castq[7])
                                if idx == 8:
                                    tensor.wait_ge(sA.h, castk[7])
                                if idx >= 4:
                                    tensor.wait_ge(sV.h, pred_ev[idx - 4])
                                elif j >= 1:
                                    tensor.wait_ge(sV.h, qk_ev[j - 1][12 + idx])
                                ins = None
                                for dt in range(DT):
                                    ins = tensor.matmul(
                                        ps_qk[idx % 4][:],
                                        wb[:, dt * D + et * P:
                                           dt * D + (et + 1) * P],
                                        xn3T[j % 2][:, dt * RB:(dt + 1) * RB],
                                        start=(dt == 0), stop=(dt == DT - 1))
                                qk_mm[j][idx] = sT.inc(ins)
                    blk.tensor(t_qk)

                    def v_ev_half(vector, j, which):
                        dstT = (qT, kT)[which]
                        for et in range(ET):
                            idx = which * 8 + et
                            vector.wait_ge(sT.h, qk_mm[j][idx])
                            qk_ev[j][idx] = sV.inc(vector.tensor_copy(
                                dstT[:, et * R + j * RB:
                                     et * R + (j + 1) * RB],
                                ps_qk[idx % 4][:]))
                    blk.vector(lambda v, j=j: v_ev_half(v, j, 0))
                    if j + 1 < NRB:
                        blk.vector(lambda v, j=j: v_comb3(v, j + 1))
                        blk.scalar(lambda sc, j=j: a_sqrt3(sc, j + 1))
                        blk.vector(lambda v, j=j: v_xn3(v, j + 1))
                    blk.vector(lambda v, j=j: v_ev_half(v, j, 1))

                    if j + 1 < NRB:
                        blk.tensor(lambda t, j=j: t_tr3(t, j + 1))
                        blk.scalar(lambda sc, j=j: a_xnT3(sc, j + 1))

                    pred_vev0 = sV.n + 1

                    def t_v(tensor, j=j, pred_vev0=pred_vev0):
                        tensor.wait_ge(sG.h, castv[7])
                        for rt in range(2):
                            if rt == 0:
                                if j >= 1:
                                    tensor.wait_ge(sV.h, v_ev[j - 1][1])
                            else:
                                tensor.wait_ge(sV.h, pred_vev0)
                            ins = None
                            for dt in range(DT):
                                for h in range(2):
                                    ins = tensor.matmul(
                                        ps_v[:, h * 512:(h + 1) * 512],
                                        xn3T[j % 2][:, dt * RB + rt * P:
                                                    dt * RB + (rt + 1) * P],
                                        wvb[:, dt * D + h * 512:
                                            dt * D + (h + 1) * 512],
                                        start=(dt == 0), stop=(dt == DT - 1))
                            v_mm[j][rt] = sT.inc(ins)
                    blk.tensor(t_v)

                    def v_vev(vector, j=j):
                        for rt in range(2):
                            vector.wait_ge(sT.h, v_mm[j][rt])
                            g = 2 * j + rt
                            v_ev[j][rt] = sV.inc(vector.tensor_copy(
                                vN[:, g * D:(g + 1) * D], ps_v[:]))
                    blk.vector(v_vev)
                    assert sV.n == pred_vev0 + 1, "p3 token prediction drift"
                    assert v_ev[j][0] == pred_vev0, "p3 vev prediction drift"

                    if j + 2 < NRB:
                        blk.sync(lambda sy, j=j: load_x1(sy, j + 2))

        if last_phase < 4:
            return nc
        wq1t5 = wbig[:, 0:DT * F]
        with ExitStack() as c4:
            x1t = [c4.enter_context(nc.sbuf_tensor(f"x1t{i}", [P, D], F32))
                   for i in range(3)]
            P_sb = [c4.enter_context(nc.sbuf_tensor(f"Psb{i}", [P, R], BF16))
                    for i in range(3)]
            PT_sb = c4.enter_context(nc.sbuf_tensor("PTsb", [P, R], BF16))
            negmax = c4.enter_context(nc.sbuf_tensor("negmax", [P, RT], F32))
            sumx = c4.enter_context(nc.sbuf_tensor("sumx", [P, RT], F32))
            rs = c4.enter_context(nc.sbuf_tensor("rs", [P, RT], F32))
            ps_s = c4.enter_context(nc.psum_tensor("ps_s", [P, R], F32))
            ps_pt = c4.enter_context(nc.psum_tensor("ps_pt", [P, R], BF16))
            ps_av = c4.enter_context(nc.psum_tensor("ps_av", [P, D], F32))

            NJT = RT
            ld_x1t = [None] * NJT
            sc_tok = [0] * NJT
            max_tok = [0] * NJT
            exp_tok = [0] * NJT
            ptt = [0] * NJT
            pte = [0] * NJT
            av_tok = [0] * NJT
            ave = [0] * NJT
            st_x2 = [None] * NJT
            exp_last_guard = [0]

            with nc.Block("p4") as blk:
                def load_x1t(sync, jt):
                    if jt >= 3:
                        wdma(sync, st_x2[jt - 3])
                    ins = sync.dma_start(out=x1t[jt % 3][:],
                                         in_=x1_hbm[jt * P:(jt + 1) * P, :])
                    ld_x1t[jt] = sL[jt % 3].tok(ins)

                def s_x01(sync):
                    load_x1t(sync, 0)
                    load_x1t(sync, 1)
                blk.sync(s_x01)

                for jt in range(NJT + 2):
                    if jt >= 2:
                        def t_pt(tensor, jt=jt):
                            jj = jt - 2
                            tensor.wait_ge(sA.h, exp_tok[jj])
                            if jj >= 1:
                                tensor.wait_ge(sV.h, pte[jj - 1])
                            for it in range(RT):
                                tensor.transpose(
                                    ps_pt[:, it * P:(it + 1) * P],
                                    P_sb[jj % 3][:, it * P:(it + 1) * P],
                                    ident_bf[:])
                            ptt[jj] = sT.inc(tensor.drain())
                        blk.tensor(t_pt)

                    if jt < NJT:
                        def t_sc(tensor, jt=jt):
                            if jt >= 1:
                                tensor.wait_ge(sA.h, exp_tok[jt - 1])
                            ins = None
                            for seg in range(4):
                                for et in range(ET):
                                    ins = tensor.matmul(
                                        ps_s[:, seg * 512:(seg + 1) * 512],
                                        kT[:, et * R + jt * P:
                                           et * R + (jt + 1) * P],
                                        qT[:, et * R + seg * 512:
                                           et * R + (seg + 1) * 512],
                                        start=(et == 0), stop=(et == ET - 1))
                            sc_tok[jt] = sT.inc(ins)
                        blk.tensor(t_sc)

                    if jt >= 2:
                        def v_pte(vector, jt=jt):
                            jj = jt - 2
                            vector.wait_ge(sT.h, ptt[jj])
                            if jj >= 1:
                                tok = av_tok[jj - 1]
                                vector.wait_ge(sT.h, tok)
                            pte[jj] = sV.inc(vector.tensor_copy(PT_sb[:],
                                                                ps_pt[:]))
                        blk.vector(v_pte)

                    if jt < NJT:
                        def v_max(vector, jt=jt):
                            vector.wait_ge(sT.h, sc_tok[jt])
                            n = sV.inc(vector.reduce_max(
                                negmax[:, jt:jt + 1], ps_s[:],
                                axis=AX.X, negate=True))
                            vector.wait_ge(sV.h, n)
                            max_tok[jt] = sV.inc(vector.tensor_scalar(
                                negmax[:, jt:jt + 1], negmax[:, jt:jt + 1],
                                1.0 / 32.0, None, ALU.mult))
                        blk.vector(v_max)

                    if jt < NJT:
                        def a_exp(scalar, jt=jt):
                            scalar.wait_ge(sV.h, max_tok[jt])
                            if jt >= 3:
                                scalar.wait_ge(sT.h, ptt[jt - 3])
                            ins = scalar.activation(
                                P_sb[jt % 3][:], ps_s[:], AF.Exp,
                                bias=negmax[:, jt:jt + 1], scale=1.0 / 32.0,
                                accum_out=sumx[:, jt:jt + 1])
                            exp_tok[jt] = sA.inc(ins)
                            if jt == NJT - 1:
                                exp_last_guard[0] = sA.inc(scalar.drain())
                        blk.scalar(a_exp)

                    if jt >= 2:
                        def t_av(tensor, jt=jt):
                            jj = jt - 2
                            tensor.wait_ge(sV.h, pte[jj])
                            if jj >= 2:
                                tensor.wait_ge(sV.h, ave[jj - 2])
                            ins = None
                            for it in range(RT):
                                for h in range(2):
                                    ins = tensor.matmul(
                                        ps_av[:, h * 512:(h + 1) * 512],
                                        PT_sb[:, it * P:(it + 1) * P],
                                        vN[:, it * D + h * 512:
                                           it * D + (h + 1) * 512],
                                        start=(it == 0), stop=(it == RT - 1))
                            av_tok[jj] = sT.inc(ins)
                        blk.tensor(t_av)

                        def v_ave(vector, jt=jt):
                            jj = jt - 2
                            guard = exp_tok[jj + 1] if jj + 1 < NJT \
                                else exp_last_guard[0]
                            vector.wait_ge(sA.h, guard)
                            n = sV.inc(vector.reciprocal(rs[:, jj:jj + 1],
                                                         sumx[:, jj:jj + 1]))
                            vector.wait_ge(sV.h, n)
                            vector.wait_ge(sT.h, av_tok[jj])
                            wdma(vector, ld_x1t[jj])
                            ave[jj] = sV.inc(vector.scalar_tensor_tensor(
                                x1t[jj % 3][:], ps_av[:], rs[:, jj:jj + 1],
                                x1t[jj % 3][:], ALU.mult, ALU.add))
                        blk.vector(v_ave)

                        def a_st(scalar, jt=jt):
                            jj = jt - 2
                            scalar.wait_ge(sV.h, ave[jj])
                            ins = scalar.dma_start(
                                out=x2_hbm[jj * P:(jj + 1) * P, :],
                                in_=x1t[jj % 3][:])
                            st_x2[jj] = (sS[0], sS[1], sS2)[jj % 3].tok(ins)
                        blk.scalar(a_st)

                    if 2 <= jt + 1 < NJT:
                        blk.sync(lambda sy, jt=jt: load_x1t(sy, jt + 1))

                def g_w15(gpsimd):
                    gpsimd.wait_ge(sT.h, sc_tok[NJT - 1])
                    ins = gpsimd.dma_start(
                        out=wq1t5.rearrange("p (a f) -> p a f", a=DT),
                        in_=wq1t_hbm.rearrange("(a p) f -> p a f", p=P))
                    sW.inc(ins, 16)
                blk.gpsimd(g_w15)

        if last_phase < 5:
            return nc
        with ExitStack() as c5:
            wq2t5 = wbig[:, 2 * ET * R:4 * ET * R]
            x2_ready = [(sS[0], sS[0].n), (sS[1], sS[1].n), (sS2, sS2.n)]
            wq2_chunks = []

            def pre_body(blk):
                def g_wq2(gpsimd):
                    for tok in x2_ready:
                        wdma(gpsimd, tok)
                    src = wq2t_hbm.rearrange("(a p) d -> p a d", p=P)
                    dst = wq2t5.rearrange("p (a d) -> p a d", a=FT)
                    sems = [sC[0], sC[1], sL[2], sL[3]]
                    for cc in range(4):
                        ins = gpsimd.dma_start(
                            out=dst[:, cc * 8:(cc + 1) * 8, :],
                            in_=src[:, cc * 8:(cc + 1) * 8, :])
                        wq2_chunks.append(sems[cc].tok(ins))
                blk.gpsimd(g_wq2)

            emit_ffn(nc, "p5", x2_hbm, out_hbm, wq1t5, wq2t5,
                     (sW, sW.n), wq2_chunks, x2_ready,
                     ones_row_bf, b1h, b2h_bf, ident_bf, eps_c,
                     sL, sS, sT, sV, sA, sG,
                     final_wait=True, pre_body=pre_body)

    return nc


_NC_CACHE = {}


def kernel(L0_attn_x, L0_attn_W_query, L0_attn_W_key, L0_attn_W_value,
           L0_ffn_W1, L0_ffn_b1, L0_ffn_W2, L0_ffn_b2):
    x = np.ascontiguousarray(np.asarray(L0_attn_x, dtype=np.float32))
    wq = np.ascontiguousarray(np.asarray(L0_attn_W_query, dtype=np.float32))
    wk = np.ascontiguousarray(np.asarray(L0_attn_W_key, dtype=np.float32))
    wv = np.ascontiguousarray(np.asarray(L0_attn_W_value, dtype=np.float32))
    w1 = np.ascontiguousarray(np.asarray(L0_ffn_W1, dtype=np.float32))
    b1 = np.ascontiguousarray(np.asarray(L0_ffn_b1, dtype=np.float32))
    w2 = np.ascontiguousarray(np.asarray(L0_ffn_W2, dtype=np.float32))
    b2 = np.ascontiguousarray(np.asarray(L0_ffn_b2, dtype=np.float32))

    if "nc" not in _NC_CACHE:
        _NC_CACHE["nc"] = build_program()
    nc = _NC_CACHE["nc"]

    in_maps = [
        {"x": x[c], "wq": wq, "wk": wk, "wv": wv,
         "w1": w1, "b1": b1, "w2": w2, "b2": b2}
        for c in range(N_CORES)
    ]
    res = run_bass_kernel_spmd(nc, in_maps, core_ids=list(range(N_CORES)))
    return np.stack([np.asarray(res.results[c]["out"], dtype=np.float32)
                     for c in range(N_CORES)], axis=0)


# version 17 (final)
# speedup vs baseline: 1.0263x; 1.0034x over previous
import numpy as np
from contextlib import ExitStack

import concourse.bass as bass
import concourse.mybir as mybir
from concourse.bass_utils import run_bass_kernel_spmd

F32 = mybir.dt.float32
BF16 = mybir.dt.bfloat16
AF = mybir.ActivationFunctionType
ALU = mybir.AluOpType
AX = mybir.AxisListType

P = 128
R = 2048
D = 1024
F = 4096
DT = D // P
FT = F // P
ET = D // P
RT = R // P
RB = 256
NRB = R // RB
EPS = 1e-5
N_CORES = 8


class Sem:
    def __init__(self, nc, ctx, name):
        self.h = ctx.enter_context(nc.semaphore(name))
        self.n = 0

    def inc(self, instr, k=1):
        instr.then_inc(self.h, k)
        self.n += k
        return self.n

    def tok(self, instr, k=16):
        return (self, self.inc(instr, k))


def wdma(eng, tok):
    eng.wait_ge(tok[0].h, tok[1])


def emit_ln_acc(scalar, xsrc, i, junk, ssum, ssq, sA):
    for rt in range(2):
        g = 2 * i + rt
        xt = xsrc[:, rt * D:(rt + 1) * D]
        n = sA.inc(scalar.activation(junk[:], xt, AF.Square,
                                     accum_out=ssq[:, g:g + 1]))
        scalar.wait_ge(sA.h, n)
        n = sA.inc(scalar.activation(junk[:], xt, AF.Copy,
                                     accum_out=ssum[:, g:g + 1]))
        scalar.wait_ge(sA.h, n)
    return sA.inc(scalar.drain())


def emit_ln_comb(v, i, acc_tok, ssum, ssq, mean, msq, vp, sV, sA):
    v.wait_ge(sA.h, acc_tok)
    n = 0
    for rt in range(2):
        g = 2 * i + rt
        n = sV.inc(v.tensor_scalar(mean[:, g:g + 1], ssum[:, g:g + 1],
                                   1.0 / 1024.0, None, ALU.mult))
        v.wait_ge(sV.h, n)
        n = sV.inc(v.scalar_tensor_tensor(msq[:, g:g + 1], mean[:, g:g + 1],
                                          1.0, mean[:, g:g + 1],
                                          ALU.mult, ALU.mult))
        v.wait_ge(sV.h, n)
        n = sV.inc(v.scalar_tensor_tensor(vp[:, g:g + 1], ssq[:, g:g + 1],
                                          1.0 / 1024.0, msq[:, g:g + 1],
                                          ALU.mult, ALU.subtract))
        v.wait_ge(sV.h, n)
    return n


def emit_ffn(nc, name, src_hbm, dst_hbm, wq1t_v, wq2t_v, wq1_ready, wq2_chunks,
             src_ready, ones_row_bf, b1h, b2h_bf, ident_bf, eps_c,
             sL, sS, sT, sV, sA, sG, final_wait=False, pre_body=None,
             st_ready=None):
    with ExitStack() as c:
        SB = lambda nm, sh, dt: c.enter_context(
            nc.sbuf_tensor(f"{name}_{nm}", sh, dt))
        xrb = [SB(f"x{i}", [P, 2 * D], F32) for i in range(2)]
        xn = [SB(f"xn{i}", [P, 2 * D], BF16) for i in range(2)]
        xnT = [SB(f"xnT{i}", [P, DT * RB], BF16) for i in range(2)]
        hT = SB("hT", [P, FT * RB], BF16)
        ssum = SB("ssum", [P, 2 * NRB], F32)
        ssq = SB("ssq", [P, 2 * NRB], F32)
        mean = SB("mean", [P, 2 * NRB], F32)
        msq = SB("msq", [P, 2 * NRB], F32)
        vp = SB("vp", [P, 2 * NRB], F32)
        rst = SB("rst", [P, 2 * NRB], F32)
        rsti = SB("rsti", [P, 2 * NRB], F32)
        sqg = SB("sqg", [P, D], F32)
        ps_h = [c.enter_context(nc.psum_tensor(f"{name}_psh{i}", [P, RB], F32))
                for i in range(4)]
        ps_oA = c.enter_context(nc.psum_tensor(f"{name}_psoA", [P, 512], F32))
        ps_oB = c.enter_context(nc.psum_tensor(f"{name}_psoB", [P, 512], F32))
        ps_x = c.enter_context(nc.psum_tensor(f"{name}_psx", [P, 2 * D], BF16))

        ld_x = [None] * NRB
        lnA = [0] * NRB
        sqd = [0] * NRB
        xnd = [0] * NRB
        trd = [0] * NRB
        xnTd = [0] * NRB
        h_mm = [0] * (NRB * FT)
        h_ev = [0] * (NRB * FT)
        o_mmA = [[0, 0] for _ in range(NRB)]
        o_mmB = [[0, 0] for _ in range(NRB)]
        st_done = [None] * NRB
        last_res = [0, 0]

        with nc.Block(name) as blk:
            if pre_body is not None:
                pre_body(blk)

            def load_x(sync, i):
                for tok in (src_ready or []):
                    wdma(sync, tok)
                if i >= 2:
                    wdma(sync, st_done[i - 2])
                src = src_hbm[i * RB:(i + 1) * RB, :] \
                    .rearrange("(t p) d -> p t d", p=P)
                ins = sync.dma_start(
                    out=xrb[i % 2][:].rearrange("p (t d) -> p t d", t=2),
                    in_=src)
                ld_x[i] = sL[i % 2].tok(ins)

            def s_x01(sync):
                load_x(sync, 0)
                load_x(sync, 1)
            blk.sync(s_x01)

            acc_tok = [0] * NRB

            def a_acc(scalar, i):
                wdma(scalar, ld_x[i])
                acc_tok[i] = emit_ln_acc(scalar, xrb[i % 2][:], i, sqg,
                                         ssum, ssq, sA)

            def v_comb(vector, i):
                lnA[i] = emit_ln_comb(vector, i, acc_tok[i], ssum, ssq,
                                      mean, msq, vp, sV, sA)

            def a_sqrt(scalar, i):
                scalar.wait_ge(sV.h, lnA[i])
                ins = scalar.activation(rst[:, 2 * i:2 * i + 2],
                                        vp[:, 2 * i:2 * i + 2], AF.Sqrt,
                                        bias=eps_c[:, 0:1], scale=4.0)
                sqd[i] = sA.inc(ins)

            def v_xn(vector, i):
                vector.wait_ge(sA.h, sqd[i])
                n = sV.inc(vector.reciprocal(rsti[:, 2 * i:2 * i + 2],
                                             rst[:, 2 * i:2 * i + 2]))
                vector.wait_ge(sV.h, n)
                if i >= 2:
                    vector.wait_ge(sT.h, trd[i - 2])
                ins = None
                for rt in range(2):
                    g = 2 * i + rt
                    ins = vector.tensor_scalar(
                        xn[i % 2][:, rt * D:(rt + 1) * D],
                        xrb[i % 2][:, rt * D:(rt + 1) * D],
                        mean[:, g:g + 1], rsti[:, g:g + 1],
                        ALU.subtract, ALU.mult)
                xnd[i] = sV.inc(ins)

            def t_tr(tensor, i):
                tensor.wait_ge(sV.h, xnd[i])
                if i >= 1:
                    tensor.wait_ge(sA.h, xnTd[i - 1])
                for rt in range(2):
                    for dt in range(DT):
                        tensor.transpose(
                            ps_x[:, rt * D + dt * P: rt * D + (dt + 1) * P],
                            xn[i % 2][:, rt * D + dt * P: rt * D + (dt + 1) * P],
                            ident_bf[:])
                trd[i] = sT.inc(tensor.drain())

            def a_xnT(scalar, i):
                scalar.wait_ge(sT.h, trd[i])
                dst = xnT[i % 2][:].rearrange("p (dt r) -> p dt r", dt=DT) \
                    .rearrange("p dt (rt c) -> p dt rt c", rt=2)
                src = ps_x[:].rearrange("p (rt dt c) -> p dt rt c", rt=2, dt=DT)
                xnTd[i] = sA.inc(scalar.copy(dst, src))

            blk.scalar(lambda sc: a_acc(sc, 0))
            blk.vector(lambda v: v_comb(v, 0))
            blk.scalar(lambda sc: a_sqrt(sc, 0))
            blk.vector(lambda v: v_xn(v, 0))
            blk.tensor(lambda t: t_tr(t, 0))
            blk.scalar(lambda sc: a_xnT(sc, 0))

            for i in range(NRB):
                if i + 1 < NRB:
                    blk.scalar(lambda sc, i=i: a_acc(sc, i + 1))

                vb = sV.n
                ln_incs = (6, 2) if i + 1 < NRB else (0, 0)
                pred_hev = [0] * FT
                for fc in range(16):
                    pred_hev[fc] = vb + 1 + fc
                for fc in range(16, 24):
                    pred_hev[fc] = vb + 16 + ln_incs[0] + (fc - 16) + 1
                for fc in range(24, FT):
                    pred_hev[fc] = vb + 24 + sum(ln_incs) + (fc - 24) + 1

                def t_mm1(tensor, i=i, pred_hev=pred_hev):
                    for fc in range(FT):
                        g = i * FT + fc
                        if fc == 0:
                            tensor.wait_ge(sA.h, xnTd[i])
                            if i == 0 and wq1_ready is not None:
                                wdma(tensor, wq1_ready)
                        if fc >= 4:
                            tensor.wait_ge(sV.h, pred_hev[fc - 4])
                        elif g >= 4:
                            tensor.wait_ge(sV.h, h_ev[g - 4])
                        ins = None
                        for dt in range(DT):
                            ins = tensor.matmul(
                                ps_h[fc % 4][:],
                                wq1t_v[:, dt * F + fc * P: dt * F + (fc + 1) * P],
                                xnT[i % 2][:, dt * RB:(dt + 1) * RB],
                                start=(dt == 0), stop=(dt == DT - 1))
                        h_mm[g] = sT.inc(ins)
                blk.tensor(t_mm1)

                def v_ev_range(vector, i, lo, hi):
                    for fc in range(lo, hi):
                        g = i * FT + fc
                        vector.wait_ge(sT.h, h_mm[g])
                        ins = vector.tensor_scalar(
                            hT[:, fc * RB:(fc + 1) * RB],
                            ps_h[fc % 4][:], b1h[:, fc:fc + 1], 0.0,
                            ALU.add, ALU.max)
                        h_ev[g] = sV.inc(ins)
                blk.vector(lambda v, i=i: v_ev_range(v, i, 0, 16))
                if i + 1 < NRB:
                    blk.vector(lambda v, i=i: v_comb(v, i + 1))
                    blk.scalar(lambda sc, i=i: a_sqrt(sc, i + 1))
                blk.vector(lambda v, i=i: v_ev_range(v, i, 16, 24))
                if i + 1 < NRB:
                    blk.vector(lambda v, i=i: v_xn(v, i + 1))
                blk.vector(lambda v, i=i: v_ev_range(v, i, 24, FT))
                assert sV.n == pred_hev[FT - 1], "mm1 evac prediction drift"

                if i + 1 < NRB:
                    blk.tensor(lambda t, i=i: t_tr(t, i + 1))
                    blk.scalar(lambda sc, i=i: a_xnT(sc, i + 1))

                def t_mm2(tensor, i=i):
                    v0 = sV.n
                    pred = [v0 + 1, v0 + 2, v0 + 3, v0 + 4]
                    tensor.wait_ge(sV.h, h_ev[i * FT + FT - 1])
                    for rt in range(2):
                        if rt == 0:
                            if last_res[0]:
                                tensor.wait_ge(sV.h, last_res[0])
                            if last_res[1]:
                                tensor.wait_ge(sV.h, last_res[1])
                        else:
                            tensor.wait_ge(sV.h, pred[0])
                            tensor.wait_ge(sV.h, pred[1])
                        for fc in range(FT):
                            if wq2_chunks is not None and i == 0 and rt == 0 \
                                    and fc % 8 == 0:
                                wdma(tensor, wq2_chunks[fc // 8])
                            lhs = hT[:, fc * RB + rt * P: fc * RB + (rt + 1) * P]
                            tensor.matmul(
                                ps_oA[:], lhs,
                                wq2t_v[:, fc * D: fc * D + 512],
                                start=(fc == 0), stop=False)
                            tensor.matmul(
                                ps_oB[:], lhs,
                                wq2t_v[:, fc * D + 512: fc * D + 1024],
                                start=(fc == 0), stop=False)
                        ins = tensor.matmul(ps_oA[:], ones_row_bf[:],
                                            b2h_bf[0:1, 0:512],
                                            start=False, stop=True)
                        o_mmA[i][rt] = sT.inc(ins)
                        ins = tensor.matmul(ps_oB[:], ones_row_bf[:],
                                            b2h_bf[0:1, 512:1024],
                                            start=False, stop=True)
                        o_mmB[i][rt] = sT.inc(ins)
                    last_res[0] = pred[2]
                    last_res[1] = pred[3]
                blk.tensor(t_mm2)

                def v_res(vector, i=i):
                    for rt in range(2):
                        vector.wait_ge(sT.h, o_mmA[i][rt])
                        base = rt * D
                        sV.inc(vector.tensor_tensor(
                            xrb[i % 2][:, base:base + 512],
                            xrb[i % 2][:, base:base + 512], ps_oA[:], ALU.add))
                        vector.wait_ge(sT.h, o_mmB[i][rt])
                        sV.inc(vector.tensor_tensor(
                            xrb[i % 2][:, base + 512:base + 1024],
                            xrb[i % 2][:, base + 512:base + 1024], ps_oB[:],
                            ALU.add))
                blk.vector(v_res)
                assert sV.n == last_res[1], "res token prediction drifted"

                def a_store(scalar, i=i):
                    scalar.wait_ge(sV.h, last_res[1])
                    if i < 2 and st_ready is not None:
                        wdma(scalar, st_ready[i])
                    dst = dst_hbm[i * RB:(i + 1) * RB, :] \
                        .rearrange("(t p) d -> p t d", p=P)
                    ins = scalar.dma_start(
                        out=dst,
                        in_=xrb[i % 2][:].rearrange("p (t d) -> p t d", t=2))
                    st_done[i] = sS[i % 2].tok(ins)
                blk.scalar(a_store)

                if i + 2 < NRB:
                    blk.sync(lambda sy, i=i: load_x(sy, i + 2))

            if final_wait:
                fa, fb = sS[0].n, sS[1].n

                def a_final(scalar):
                    scalar.wait_ge(sS[0].h, fa)
                    scalar.wait_ge(sS[1].h, fb)
                blk.scalar(a_final)


def build_program(last_phase=5):
    nc = bass.Bass("TRN2", target_bir_lowering=False)

    x_in = nc.dram_tensor("x", [R, D], F32, kind="ExternalInput").ap()
    wq_in = nc.dram_tensor("wq", [D, D], F32, kind="ExternalInput").ap()
    wk_in = nc.dram_tensor("wk", [D, D], F32, kind="ExternalInput").ap()
    wv_in = nc.dram_tensor("wv", [D, D], F32, kind="ExternalInput").ap()
    w1_in = nc.dram_tensor("w1", [F, D], F32, kind="ExternalInput").ap()
    b1_in = nc.dram_tensor("b1", [F], F32, kind="ExternalInput").ap()
    w2_in = nc.dram_tensor("w2", [D, F], F32, kind="ExternalInput").ap()
    b2_in = nc.dram_tensor("b2", [D], F32, kind="ExternalInput").ap()
    out_hbm = nc.dram_tensor("out", [R, D], F32, kind="ExternalOutput").ap()

    wq1t_hbm = nc.dram_tensor("wq1t_s", [D, F], BF16, kind="Internal").ap()
    wq2t_hbm = nc.dram_tensor("wq2t_s", [F, D], BF16, kind="Internal").ap()
    x1_hbm = nc.dram_tensor("x1_s", [R, D], F32, kind="Internal").ap()
    x2_hbm = nc.dram_tensor("x2_s", [R, D], F32, kind="Internal").ap()

    with ExitStack() as ctx:
        SB = lambda name, shape, dt: ctx.enter_context(
            nc.sbuf_tensor(name, shape, dt))

        ones_col = SB("ones_col", [P, 1], F32)
        ones_row = SB("ones_row", [1, P], F32)
        ones_row_bf = SB("ones_row_bf", [1, P], BF16)
        ident_bf = SB("ident_bf", [P, P], BF16)
        nident_bf = SB("nident_bf", [P, P], BF16)
        ident_f32 = SB("ident_f32", [P, P], F32)
        b1h = SB("b1h", [P, FT], F32)
        b2h_bf = SB("b2h_bf", [1, D], BF16)
        s12b = SB("s12b", [P, 2], F32)
        eps_c = SB("eps_c", [P, 2], F32)

        sL = [Sem(nc, ctx, f"sL{i}") for i in range(4)]
        sS = [Sem(nc, ctx, f"sS{i}") for i in range(2)]
        sC = [Sem(nc, ctx, f"sC{i}") for i in range(2)]
        sT = Sem(nc, ctx, "sT")
        sV = Sem(nc, ctx, "sV")
        sA = Sem(nc, ctx, "sA")
        sG = Sem(nc, ctx, "sG")
        sW = Sem(nc, ctx, "sW")
        sS2 = Sem(nc, ctx, "sS2")
        sQ = [Sem(nc, ctx, f"sQ{i}") for i in range(4)]

        with ExitStack() as c0:
            gstage = [c0.enter_context(nc.sbuf_tensor(f"gst{i}", [P, 8192], F32))
                      for i in range(2)]
            gacc = c0.enter_context(nc.sbuf_tensor("gacc", [P, 8], F32))
            gtot = c0.enter_context(nc.sbuf_tensor("gtot", [P, 2], F32))
            sinv = c0.enter_context(nc.sbuf_tensor("sinv", [1, 2], F32))
            b1_t32 = c0.enter_context(nc.sbuf_tensor("b1_t32", [32, P], F32))
            b2raw = c0.enter_context(nc.sbuf_tensor("b2raw", [1, D], F32))
            ps0 = c0.enter_context(nc.psum_tensor("ps0", [P, 2], F32))
            ps0b = c0.enter_context(nc.psum_tensor("ps0b", [P, 32], F32))

            w1_c = w1_in.rearrange("(c t p) d -> c p t d", p=P, t=8)
            w2_c = w2_in.rearrange("(c t p) f -> c p t f", p=P, t=2)

            ld_chunk = [None] * 8
            ld_b = [None, None]
            red_done = [0] * 8

            with nc.Block("p0") as blk:
                def load_chunk(sync, k):
                    if k < 4:
                        src = w1_c[k]
                        dst = gstage[k % 2][:].rearrange("p (t d) -> p t d", t=8)
                    else:
                        src = w2_c[k - 4]
                        dst = gstage[k % 2][:].rearrange("p (t f) -> p t f", t=2)
                    ins = sync.dma_start(out=dst, in_=src)
                    ld_chunk[k] = sL[k % 2].tok(ins)

                def s_first(sync):
                    load_chunk(sync, 0)
                    load_chunk(sync, 1)
                    ins = sync.dma_start(
                        out=b1_t32[:], in_=b1_in.rearrange("(c p) -> c p", p=P))
                    ld_b[0] = sL[2].tok(ins)
                    ins = sync.dma_start(out=b2raw[:], in_=b2_in[None, :])
                    ld_b[1] = sL[3].tok(ins)
                blk.sync(s_first)

                def gp_body(gpsimd):
                    gpsimd.memset(ones_col[:], 1.0)
                    gpsimd.memset(ones_row[:], 1.0)
                    gpsimd.memset(ones_row_bf[:], 1.0)
                    gpsimd.memset(eps_c[:, 0:1], 4.0 * EPS)
                    gpsimd.memset(eps_c[:, 1:2], EPS)
                    gpsimd.memset(ident_bf[:], 0.0)
                    gpsimd.memset(nident_bf[:], 0.0)
                    n = sG.inc(gpsimd.memset(ident_f32[:], 0.0))
                    gpsimd.wait_ge(sG.h, n)
                    gpsimd.affine_select(
                        out=ident_bf[:], in_=ident_bf[:],
                        compare_op=ALU.not_equal, fill=1.0, base=0,
                        pattern=[[-1, P]], channel_multiplier=1)
                    gpsimd.affine_select(
                        out=nident_bf[:], in_=nident_bf[:],
                        compare_op=ALU.not_equal, fill=-1.0, base=0,
                        pattern=[[-1, P]], channel_multiplier=1)
                    gpsimd.affine_select(
                        out=ident_f32[:], in_=ident_f32[:],
                        compare_op=ALU.not_equal, fill=1.0, base=0,
                        pattern=[[-1, P]], channel_multiplier=1)
                    sG.inc(gpsimd.drain())
                blk.gpsimd(gp_body)
                g_ready = sG.n

                for k in range(8):
                    def v_red(vector, k=k):
                        wdma(vector, ld_chunk[k])
                        ins = vector.reduce_sum(gacc[:, k:k + 1],
                                                gstage[k % 2][:], axis=AX.X,
                                                apply_absolute_value=True)
                        red_done[k] = sV.inc(ins)
                    blk.vector(v_red)
                    if k + 2 < 8:
                        def s_pref(sync, k=k):
                            sync.wait_ge(sV.h, red_done[k])
                            load_chunk(sync, k + 2)
                        blk.sync(s_pref)

                def v_tot(vector):
                    vector.wait_ge(sV.h, red_done[7])
                    vector.reduce_sum(gtot[:, 0:1], gacc[:, 0:4], axis=AX.X)
                    sV.inc(vector.reduce_sum(gtot[:, 1:2], gacc[:, 4:8],
                                             axis=AX.X))
                blk.vector(v_tot)
                v_tot_n = sV.n

                def t_sum(tensor):
                    tensor.wait_ge(sV.h, v_tot_n)
                    tensor.wait_ge(sG.h, g_ready)
                    ins = tensor.matmul(ps0[0:1, 0:2], ones_col[:], gtot[:],
                                        start=True, stop=True)
                    sT.inc(ins)
                blk.tensor(t_sum)
                t_sum_n = sT.n

                def v_sinv(vector):
                    vector.wait_ge(sT.h, t_sum_n)
                    n = sV.inc(vector.tensor_copy(sinv[:], ps0[0:1, 0:2]))
                    vector.wait_ge(sV.h, n)
                    n = sV.inc(vector.tensor_scalar(
                        sinv[:], sinv[:], 1.0 / 4194304.0, EPS,
                        ALU.mult, ALU.add))
                    vector.wait_ge(sV.h, n)
                    sV.inc(vector.reciprocal(sinv[:], sinv[:]))
                blk.vector(v_sinv)
                v_sinv_n = sV.n

                def t_bc(tensor):
                    tensor.wait_ge(sV.h, v_sinv_n)
                    tensor.matmul(ps0[:, 0:2], ones_row[:], sinv[:],
                                  start=True, stop=True)
                    wdma(tensor, ld_b[0])
                    tensor.transpose(ps0b[:, 0:32], b1_t32[:],
                                     ident_f32[0:32, 0:32])
                    sT.inc(tensor.drain())
                blk.tensor(t_bc)
                t_bc_n = sT.n

                def v_fin(vector):
                    vector.wait_ge(sT.h, t_bc_n)
                    vector.tensor_copy(s12b[:], ps0[:, 0:2])
                    vector.tensor_scalar(b1h[:], ps0b[:, 0:32], 0.5, None,
                                         ALU.mult)
                    wdma(vector, ld_b[1])
                    sV.inc(vector.tensor_scalar(b2h_bf[:], b2raw[:], 0.5, None,
                                                ALU.mult))
                blk.vector(v_fin)

        if last_phase < 1:
            return nc
        with ExitStack() as c12:
            wq1t = c12.enter_context(nc.sbuf_tensor("wq1t", [P, DT * F], BF16))
            wq2t = c12.enter_context(nc.sbuf_tensor("wq2t", [P, FT * D], BF16))
            wq1t_vv = wq1t[:].rearrange("p (dt f) -> p dt f", dt=DT)
            wq2t_vv = wq2t[:].rearrange("p (ft d) -> p ft d", ft=FT)

            with ExitStack() as c1:
                wstage = [c1.enter_context(nc.sbuf_tensor(f"wst{i}", [P, 1024], F32))
                          for i in range(2)]
                a_sb = [c1.enter_context(nc.sbuf_tensor(f"qa{i}", [P, 1024], BF16))
                        for i in range(2)]
                b_sb = [c1.enter_context(nc.sbuf_tensor(f"qb{i}", [P, 1024], BF16))
                        for i in range(2)]
                pswf = [c1.enter_context(nc.psum_tensor(f"pswf{i}", [P, 1024], F32))
                        for i in range(2)]

                NTILES = 64
                ld_w = [None] * NTILES
                ab_done = [0] * NTILES
                tr_done = [0] * NTILES
                ev_done = [0] * NTILES

                def w_src(t):
                    if t < 32:
                        return w1_in[t * P:(t + 1) * P, :]
                    u = t - 32
                    dt, fs = u // 4, u % 4
                    return w2_in[dt * P:(dt + 1) * P, fs * 1024:(fs + 1) * 1024]

                with nc.Block("p1") as blk:
                    def load_w(sync, t):
                        if t >= 2:
                            sync.wait_ge(sV.h, ab_done[t - 2])
                        ins = sync.dma_start(out=wstage[t % 2][:], in_=w_src(t))
                        ld_w[t] = sL[t % 2].tok(ins)

                    def s_first(sync):
                        load_w(sync, 0)
                        load_w(sync, 1)
                    blk.sync(s_first)

                    for t in range(NTILES):
                        sc = s12b[:, 0:1] if t < 32 else s12b[:, 1:2]

                        def v_ab(vector, t=t, sc=sc):
                            wdma(vector, ld_w[t])
                            if t >= 2:
                                vector.wait_ge(sT.h, tr_done[t - 2])
                            vector.tensor_scalar(
                                a_sb[t % 2][:], wstage[t % 2][:], sc, 0.5,
                                ALU.mult, ALU.is_ge)
                            ab_done[t] = sV.inc(vector.tensor_scalar(
                                b_sb[t % 2][:], wstage[t % 2][:], sc, -0.5,
                                ALU.mult, ALU.is_le))
                        blk.vector(v_ab)

                        def t_q(tensor, t=t):
                            tensor.wait_ge(sV.h, ab_done[t])
                            if t >= 2:
                                tensor.wait_ge(sA.h, ev_done[t - 2])
                            ins = None
                            for j in range(8):
                                sl = pswf[t % 2][:, j * P:(j + 1) * P]
                                tensor.matmul(sl, a_sb[t % 2][:, j * P:(j + 1) * P],
                                              ident_bf[:], start=True, stop=False)
                                ins = tensor.matmul(
                                    sl, b_sb[t % 2][:, j * P:(j + 1) * P],
                                    nident_bf[:], start=False, stop=True)
                            tr_done[t] = sT.inc(ins)
                        blk.tensor(t_q)

                        if t < 32:
                            dest = wq1t_vv[:, :, t * P:(t + 1) * P]
                        else:
                            u = t - 32
                            dt, fs = u // 4, u % 4
                            dest = wq2t_vv[:, fs * 8:(fs + 1) * 8,
                                           dt * P:(dt + 1) * P]
                        src = pswf[t % 2][:].rearrange("p (j c) -> p j c", j=8)

                        def a_ev(scalar, t=t, dest=dest, src=src):
                            scalar.wait_ge(sT.h, tr_done[t])
                            ev_done[t] = sA.inc(scalar.copy(dest, src))
                        blk.scalar(a_ev)

                        if t + 2 < NTILES:
                            blk.sync(lambda sy, t=t: load_w(sy, t + 2))

                    hd = sV.n
                    ha = sA.n

                    wstore_toks = []

                    def a_wstore(scalar):
                        scalar.wait_ge(sV.h, hd)
                        scalar.wait_ge(sA.h, ha)
                        ins = scalar.dma_start(
                            out=wq1t_hbm.rearrange("(a p) f -> p a f", p=P),
                            in_=wq1t_vv)
                        wstore_toks.append(sS[0].tok(ins))
                        ins = scalar.dma_start(
                            out=wq2t_hbm.rearrange("(a p) d -> p a d", p=P),
                            in_=wq2t_vv)
                        wstore_toks.append(sS[1].tok(ins))
                    blk.scalar(a_wstore)

            if last_phase < 2:
                return nc
            emit_ffn(nc, "p2", x_in, x1_hbm, wq1t[:], wq2t[:], None, None, None,
                     ones_row_bf, b1h, b2h_bf, ident_bf, eps_c,
                     sL, sS, sT, sV, sA, sG, st_ready=wstore_toks)

        if last_phase < 3:
            return nc
        cW = ctx.enter_context(nc.sbuf_tensor("wbig", [P, 4 * ET * R], BF16))
        wbig = cW[:]
        qT = wbig[:, 0 * ET * R:1 * ET * R]
        kT = wbig[:, 1 * ET * R:2 * ET * R]
        vN = wbig[:, 2 * ET * R:3 * ET * R]
        wqb = wbig[:, 3 * ET * R:3 * ET * R + DT * D]
        wkb = wbig[:, 3 * ET * R + DT * D:4 * ET * R]

        with ExitStack() as c3:
            SB3 = lambda nm, sh, dt: c3.enter_context(nc.sbuf_tensor(nm, sh, dt))
            wvb = SB3("wvb", [P, DT * D], BF16)
            wst3 = [SB3(f"wst3{i}", [P, 1024], F32) for i in range(2)]
            wstv = [SB3(f"wstv{i}", [P, 1024], F32) for i in range(2)]
            xrb3 = [SB3(f"xrb3{i}", [P, 2 * D], F32) for i in range(2)]
            xn3 = [SB3(f"xn3{i}", [P, 2 * D], BF16) for i in range(2)]
            xn3T = [SB3(f"xn3T{i}", [P, DT * RB], BF16) for i in range(2)]
            ssum3 = SB3("ssum3", [P, 2 * NRB], F32)
            ssq3 = SB3("ssq3", [P, 2 * NRB], F32)
            mean3 = SB3("mean3", [P, 2 * NRB], F32)
            msq3 = SB3("msq3", [P, 2 * NRB], F32)
            vp3 = SB3("vp3", [P, 2 * NRB], F32)
            rst3 = SB3("rst3", [P, 2 * NRB], F32)
            rsti3 = SB3("rsti3", [P, 2 * NRB], F32)
            sqg3 = SB3("sqg3", [P, D], F32)
            ps_qk = [c3.enter_context(nc.psum_tensor(f"psqk{i}", [P, RB], F32))
                     for i in range(4)]
            ps_v = c3.enter_context(nc.psum_tensor("psv", [P, D], F32))
            ps_x3 = c3.enter_context(nc.psum_tensor("psx3", [P, 2 * D], BF16))

            x1_ready = [(sS[0], sS[0].n), (sS[1], sS[1].n)]
            ld_q = [None] * 8
            ld_k = [None] * 8
            ld_v = [None] * 8
            castq = [0] * 8
            castk = [0] * 8
            castv = [0] * 8
            ld_x1 = [None] * NRB
            lnA3 = [0] * NRB
            sqd3 = [0] * NRB
            xnd3 = [0] * NRB
            trd3 = [0] * NRB
            xnT3d = [0] * NRB
            qk_mm = [[0] * 16 for _ in range(NRB)]
            qk_ev = [[0] * 16 for _ in range(NRB)]
            v_mm = [[0, 0] for _ in range(NRB)]
            v_ev = [[0, 0] for _ in range(NRB)]

            with nc.Block("p3") as blk:
                def load_qk(gq, n):
                    if n >= 2:
                        prev = castq[n - 2] if n - 2 < 8 else castk[n - 10]
                        gq.wait_ge((sV if n - 2 < 8 else sA).h, prev)
                    w_ap = wq_in if n < 8 else wk_in
                    ch = n % 8
                    ins = gq.dma_start(out=wst3[n % 2][:],
                                       in_=w_ap[ch * P:(ch + 1) * P, :])
                    if n < 8:
                        ld_q[ch] = sC[n % 2].tok(ins)
                    else:
                        ld_k[ch] = sC[n % 2].tok(ins)

                def load_v(sync, ch):
                    if ch >= 2:
                        sync.wait_ge(sG.h, castv[ch - 2])
                    ins = sync.dma_start(out=wstv[ch % 2][:],
                                         in_=wv_in[ch * P:(ch + 1) * P, :])
                    ld_v[ch] = sL[2 + ch % 2].tok(ins)

                def load_x1(sync, j):
                    for tok in x1_ready:
                        wdma(sync, tok)
                    if j >= 2:
                        sync.wait_ge(sV.h, lnA3[j - 2])
                        sync.wait_ge(sV.h, xnd3[j - 2])
                    src = x1_hbm[j * RB:(j + 1) * RB, :] \
                        .rearrange("(t p) d -> p t d", p=P)
                    ins = sync.dma_start(
                        out=xrb3[j % 2][:].rearrange("p (t d) -> p t d", t=2),
                        in_=src)
                    ld_x1[j] = sL[j % 2].tok(ins)

                def s_x01(sync):
                    load_x1(sync, 0)
                    load_x1(sync, 1)
                    load_v(sync, 0)
                    load_v(sync, 1)
                blk.sync(s_x01)

                def g_w01(gpsimd):
                    for tok in x1_ready:
                        wdma(gpsimd, tok)
                    load_qk(gpsimd, 0)
                    load_qk(gpsimd, 1)
                blk.gpsimd(g_w01)

                acc3_tok = [0] * NRB

                def a_acc3(scalar, j):
                    wdma(scalar, ld_x1[j])
                    acc3_tok[j] = emit_ln_acc(scalar, xrb3[j % 2][:], j, sqg3,
                                              ssum3, ssq3, sA)

                def v_comb3(vector, j):
                    lnA3[j] = emit_ln_comb(vector, j, acc3_tok[j], ssum3,
                                           ssq3, mean3, msq3, vp3, sV, sA)

                def a_sqrt3(scalar, j):
                    scalar.wait_ge(sV.h, lnA3[j])
                    ins = scalar.activation(rst3[:, 2 * j:2 * j + 2],
                                            vp3[:, 2 * j:2 * j + 2], AF.Sqrt,
                                            bias=eps_c[:, 1:2])
                    sqd3[j] = sA.inc(ins)

                def v_xn3(vector, j):
                    vector.wait_ge(sA.h, sqd3[j])
                    n = sV.inc(vector.reciprocal(rsti3[:, 2 * j:2 * j + 2],
                                                 rst3[:, 2 * j:2 * j + 2]))
                    vector.wait_ge(sV.h, n)
                    if j >= 2:
                        vector.wait_ge(sT.h, trd3[j - 2])
                    ins = None
                    for rt in range(2):
                        g = 2 * j + rt
                        ins = vector.tensor_scalar(
                            xn3[j % 2][:, rt * D:(rt + 1) * D],
                            xrb3[j % 2][:, rt * D:(rt + 1) * D],
                            mean3[:, g:g + 1], rsti3[:, g:g + 1],
                            ALU.subtract, ALU.mult)
                    xnd3[j] = sV.inc(ins)

                def t_tr3(tensor, j):
                    tensor.wait_ge(sV.h, xnd3[j])
                    if j >= 1:
                        tensor.wait_ge(sA.h, xnT3d[j - 1])
                    for rt in range(2):
                        for dt in range(DT):
                            tensor.transpose(
                                ps_x3[:, rt * D + dt * P: rt * D + (dt + 1) * P],
                                xn3[j % 2][:, rt * D + dt * P:
                                            rt * D + (dt + 1) * P],
                                ident_bf[:])
                    trd3[j] = sT.inc(tensor.drain())

                def a_xnT3(scalar, j):
                    scalar.wait_ge(sT.h, trd3[j])
                    dst = xn3T[j % 2][:].rearrange("p (dt r) -> p dt r", dt=DT) \
                        .rearrange("p dt (rt c) -> p dt rt c", rt=2)
                    src = ps_x3[:].rearrange("p (rt dt c) -> p dt rt c",
                                             rt=2, dt=DT)
                    xnT3d[j] = sA.inc(scalar.copy(dst, src))

                blk.scalar(lambda sc: a_acc3(sc, 0))
                blk.vector(lambda v: v_comb3(v, 0))
                blk.scalar(lambda sc: a_sqrt3(sc, 0))
                blk.vector(lambda v: v_xn3(v, 0))
                blk.tensor(lambda t: t_tr3(t, 0))
                blk.scalar(lambda sc: a_xnT3(sc, 0))

                for n in range(16):
                    if n < 8:
                        def v_cq(vector, n=n):
                            wdma(vector, ld_q[n])
                            castq[n] = sV.inc(vector.tensor_copy(
                                wqb[:, n * D:(n + 1) * D], wst3[n % 2][:]))
                        blk.vector(v_cq)
                    else:
                        def a_ck(scalar, n=n):
                            ch = n - 8
                            wdma(scalar, ld_k[ch])
                            castk[ch] = sA.inc(scalar.copy(
                                wkb[:, ch * D:(ch + 1) * D], wst3[n % 2][:]))
                        blk.scalar(a_ck)
                    if n + 2 < 16:
                        blk.gpsimd(lambda g, n=n: load_qk(g, n + 2))
                for ch in range(8):
                    def g_cv(gpsimd, ch=ch):
                        wdma(gpsimd, ld_v[ch])
                        castv[ch] = sG.inc(gpsimd.tensor_copy(
                            wvb[:, ch * D:(ch + 1) * D], wstv[ch % 2][:]))
                    blk.gpsimd(g_cv)
                    if ch + 2 < 8:
                        blk.sync(lambda sy, ch=ch: load_v(sy, ch + 2))

                for j in range(NRB):
                    if j + 1 < NRB:
                        blk.scalar(lambda sc, j=j: a_acc3(sc, j + 1))

                    vbase = sV.n
                    ln_incs = 8 if j + 1 < NRB else 0
                    pred_ev = [0] * 16
                    for idx in range(8):
                        pred_ev[idx] = vbase + 1 + idx
                    for idx in range(8, 16):
                        pred_ev[idx] = vbase + 8 + ln_incs + (idx - 8) + 1

                    def t_qk(tensor, j=j, pred_ev=pred_ev):
                        for which in range(2):
                            wb = (wqb, wkb)[which]
                            for et in range(ET):
                                idx = which * 8 + et
                                if idx == 0:
                                    tensor.wait_ge(sA.h, xnT3d[j])
                                    tensor.wait_ge(sV.h, castq[7])
                                if idx == 8:
                                    tensor.wait_ge(sA.h, castk[7])
                                if idx >= 4:
                                    tensor.wait_ge(sV.h, pred_ev[idx - 4])
                                elif j >= 1:
                                    tensor.wait_ge(sV.h, qk_ev[j - 1][12 + idx])
                                ins = None
                                for dt in range(DT):
                                    ins = tensor.matmul(
                                        ps_qk[idx % 4][:],
                                        wb[:, dt * D + et * P:
                                           dt * D + (et + 1) * P],
                                        xn3T[j % 2][:, dt * RB:(dt + 1) * RB],
                                        start=(dt == 0), stop=(dt == DT - 1))
                                qk_mm[j][idx] = sT.inc(ins)
                    blk.tensor(t_qk)

                    def v_ev_half(vector, j, which):
                        dstT = (qT, kT)[which]
                        for et in range(ET):
                            idx = which * 8 + et
                            vector.wait_ge(sT.h, qk_mm[j][idx])
                            qk_ev[j][idx] = sV.inc(vector.tensor_copy(
                                dstT[:, et * R + j * RB:
                                     et * R + (j + 1) * RB],
                                ps_qk[idx % 4][:]))
                    blk.vector(lambda v, j=j: v_ev_half(v, j, 0))
                    if j + 1 < NRB:
                        blk.vector(lambda v, j=j: v_comb3(v, j + 1))
                        blk.scalar(lambda sc, j=j: a_sqrt3(sc, j + 1))
                        blk.vector(lambda v, j=j: v_xn3(v, j + 1))
                    blk.vector(lambda v, j=j: v_ev_half(v, j, 1))

                    if j + 1 < NRB:
                        blk.tensor(lambda t, j=j: t_tr3(t, j + 1))
                        blk.scalar(lambda sc, j=j: a_xnT3(sc, j + 1))

                    pred_vev0 = sV.n + 1

                    def t_v(tensor, j=j, pred_vev0=pred_vev0):
                        tensor.wait_ge(sG.h, castv[7])
                        for rt in range(2):
                            if rt == 0:
                                if j >= 1:
                                    tensor.wait_ge(sV.h, v_ev[j - 1][1])
                            else:
                                tensor.wait_ge(sV.h, pred_vev0)
                            ins = None
                            for dt in range(DT):
                                for h in range(2):
                                    ins = tensor.matmul(
                                        ps_v[:, h * 512:(h + 1) * 512],
                                        xn3T[j % 2][:, dt * RB + rt * P:
                                                    dt * RB + (rt + 1) * P],
                                        wvb[:, dt * D + h * 512:
                                            dt * D + (h + 1) * 512],
                                        start=(dt == 0), stop=(dt == DT - 1))
                            v_mm[j][rt] = sT.inc(ins)
                    blk.tensor(t_v)

                    def v_vev(vector, j=j):
                        for rt in range(2):
                            vector.wait_ge(sT.h, v_mm[j][rt])
                            g = 2 * j + rt
                            v_ev[j][rt] = sV.inc(vector.tensor_copy(
                                vN[:, g * D:(g + 1) * D], ps_v[:]))
                    blk.vector(v_vev)
                    assert sV.n == pred_vev0 + 1, "p3 token prediction drift"
                    assert v_ev[j][0] == pred_vev0, "p3 vev prediction drift"

                    if j + 2 < NRB:
                        blk.sync(lambda sy, j=j: load_x1(sy, j + 2))

        if last_phase < 4:
            return nc
        wq1t5 = wbig[:, 0:DT * F]
        with ExitStack() as c4:
            x1t = [c4.enter_context(nc.sbuf_tensor(f"x1t{i}", [P, D], F32))
                   for i in range(3)]
            P_sb = [c4.enter_context(nc.sbuf_tensor(f"Psb{i}", [P, R], BF16))
                    for i in range(3)]
            PT_sb = c4.enter_context(nc.sbuf_tensor("PTsb", [P, R], BF16))
            negmax = c4.enter_context(nc.sbuf_tensor("negmax", [P, RT], F32))
            sumx = c4.enter_context(nc.sbuf_tensor("sumx", [P, RT], F32))
            rs = c4.enter_context(nc.sbuf_tensor("rs", [P, RT], F32))
            ps_s = c4.enter_context(nc.psum_tensor("ps_s", [P, R], F32))
            ps_pt = c4.enter_context(nc.psum_tensor("ps_pt", [P, R], BF16))
            ps_av = c4.enter_context(nc.psum_tensor("ps_av", [P, D], F32))

            NJT = RT
            ld_x1t = [None] * NJT
            sc_tok = [0] * NJT
            max_tok = [0] * NJT
            exp_tok = [0] * NJT
            ptt = [0] * NJT
            pte = [0] * NJT
            av_tok = [0] * NJT
            ave = [0] * NJT
            st_x2 = [None] * NJT
            exp_last_guard = [0]

            with nc.Block("p4") as blk:
                def load_x1t(sync, jt):
                    if jt >= 3:
                        wdma(sync, st_x2[jt - 3])
                    ins = sync.dma_start(out=x1t[jt % 3][:],
                                         in_=x1_hbm[jt * P:(jt + 1) * P, :])
                    ld_x1t[jt] = sL[jt % 3].tok(ins)

                def s_x01(sync):
                    load_x1t(sync, 0)
                    load_x1t(sync, 1)
                blk.sync(s_x01)

                for jt in range(NJT + 2):
                    if jt >= 2:
                        def t_pt(tensor, jt=jt):
                            jj = jt - 2
                            tensor.wait_ge(sA.h, exp_tok[jj])
                            if jj >= 1:
                                tensor.wait_ge(sV.h, pte[jj - 1])
                            for it in range(RT):
                                tensor.transpose(
                                    ps_pt[:, it * P:(it + 1) * P],
                                    P_sb[jj % 3][:, it * P:(it + 1) * P],
                                    ident_bf[:])
                            ptt[jj] = sT.inc(tensor.drain())
                        blk.tensor(t_pt)

                    if jt < NJT:
                        def t_sc(tensor, jt=jt):
                            if jt >= 1:
                                tensor.wait_ge(sA.h, exp_tok[jt - 1])
                            ins = None
                            for seg in range(4):
                                for et in range(ET):
                                    ins = tensor.matmul(
                                        ps_s[:, seg * 512:(seg + 1) * 512],
                                        kT[:, et * R + jt * P:
                                           et * R + (jt + 1) * P],
                                        qT[:, et * R + seg * 512:
                                           et * R + (seg + 1) * 512],
                                        start=(et == 0), stop=(et == ET - 1))
                            sc_tok[jt] = sT.inc(ins)
                        blk.tensor(t_sc)

                    if jt >= 2:
                        def v_pte(vector, jt=jt):
                            jj = jt - 2
                            vector.wait_ge(sT.h, ptt[jj])
                            if jj >= 1:
                                tok = av_tok[jj - 1]
                                vector.wait_ge(sT.h, tok)
                            pte[jj] = sV.inc(vector.tensor_copy(PT_sb[:],
                                                                ps_pt[:]))
                        blk.vector(v_pte)

                    if jt < NJT:
                        def v_max(vector, jt=jt):
                            vector.wait_ge(sT.h, sc_tok[jt])
                            n = sV.inc(vector.reduce_max(
                                negmax[:, jt:jt + 1], ps_s[:],
                                axis=AX.X, negate=True))
                            vector.wait_ge(sV.h, n)
                            max_tok[jt] = sV.inc(vector.tensor_scalar(
                                negmax[:, jt:jt + 1], negmax[:, jt:jt + 1],
                                1.0 / 32.0, None, ALU.mult))
                        blk.vector(v_max)

                    if jt < NJT:
                        def a_exp(scalar, jt=jt):
                            scalar.wait_ge(sV.h, max_tok[jt])
                            if jt >= 3:
                                scalar.wait_ge(sT.h, ptt[jt - 3])
                            ins = scalar.activation(
                                P_sb[jt % 3][:], ps_s[:], AF.Exp,
                                bias=negmax[:, jt:jt + 1], scale=1.0 / 32.0,
                                accum_out=sumx[:, jt:jt + 1])
                            exp_tok[jt] = sA.inc(ins)
                            if jt == NJT - 1:
                                exp_last_guard[0] = sA.inc(scalar.drain())
                        blk.scalar(a_exp)

                    if jt >= 2:
                        def t_av(tensor, jt=jt):
                            jj = jt - 2
                            tensor.wait_ge(sV.h, pte[jj])
                            if jj >= 2:
                                tensor.wait_ge(sV.h, ave[jj - 2])
                            ins = None
                            for it in range(RT):
                                for h in range(2):
                                    ins = tensor.matmul(
                                        ps_av[:, h * 512:(h + 1) * 512],
                                        PT_sb[:, it * P:(it + 1) * P],
                                        vN[:, it * D + h * 512:
                                           it * D + (h + 1) * 512],
                                        start=(it == 0), stop=(it == RT - 1))
                            av_tok[jj] = sT.inc(ins)
                        blk.tensor(t_av)

                        def v_ave(vector, jt=jt):
                            jj = jt - 2
                            guard = exp_tok[jj + 1] if jj + 1 < NJT \
                                else exp_last_guard[0]
                            vector.wait_ge(sA.h, guard)
                            n = sV.inc(vector.reciprocal(rs[:, jj:jj + 1],
                                                         sumx[:, jj:jj + 1]))
                            vector.wait_ge(sV.h, n)
                            vector.wait_ge(sT.h, av_tok[jj])
                            wdma(vector, ld_x1t[jj])
                            ave[jj] = sV.inc(vector.scalar_tensor_tensor(
                                x1t[jj % 3][:], ps_av[:], rs[:, jj:jj + 1],
                                x1t[jj % 3][:], ALU.mult, ALU.add))
                        blk.vector(v_ave)

                        def a_st(scalar, jt=jt):
                            jj = jt - 2
                            scalar.wait_ge(sV.h, ave[jj])
                            ins = scalar.dma_start(
                                out=x2_hbm[jj * P:(jj + 1) * P, :],
                                in_=x1t[jj % 3][:])
                            st_x2[jj] = (sS[0], sS[1], sS2)[jj % 3].tok(ins)
                        blk.scalar(a_st)

                    if 2 <= jt + 1 < NJT:
                        blk.sync(lambda sy, jt=jt: load_x1t(sy, jt + 1))

                def g_w15(gpsimd):
                    gpsimd.wait_ge(sT.h, sc_tok[NJT - 1])
                    ins = gpsimd.dma_start(
                        out=wq1t5.rearrange("p (a f) -> p a f", a=DT),
                        in_=wq1t_hbm.rearrange("(a p) f -> p a f", p=P))
                    sW.inc(ins, 16)
                blk.gpsimd(g_w15)

        if last_phase < 5:
            return nc
        with ExitStack() as c5:
            wq2t5 = wbig[:, 2 * ET * R:4 * ET * R]
            x2_ready = [(sS[0], sS[0].n), (sS[1], sS[1].n), (sS2, sS2.n)]
            wq2_chunks = []

            def pre_body(blk):
                def g_wq2(gpsimd):
                    for tok in x2_ready:
                        wdma(gpsimd, tok)
                    src = wq2t_hbm.rearrange("(a p) d -> p a d", p=P)
                    dst = wq2t5.rearrange("p (a d) -> p a d", a=FT)
                    sems = sQ
                    for cc in range(4):
                        ins = gpsimd.dma_start(
                            out=dst[:, cc * 8:(cc + 1) * 8, :],
                            in_=src[:, cc * 8:(cc + 1) * 8, :])
                        wq2_chunks.append(sems[cc].tok(ins))
                blk.gpsimd(g_wq2)

            emit_ffn(nc, "p5", x2_hbm, out_hbm, wq1t5, wq2t5,
                     (sW, sW.n), wq2_chunks, x2_ready,
                     ones_row_bf, b1h, b2h_bf, ident_bf, eps_c,
                     sL, sS, sT, sV, sA, sG,
                     final_wait=True, pre_body=pre_body)

    return nc


_NC_CACHE = {}


def kernel(L0_attn_x, L0_attn_W_query, L0_attn_W_key, L0_attn_W_value,
           L0_ffn_W1, L0_ffn_b1, L0_ffn_W2, L0_ffn_b2):
    x = np.ascontiguousarray(np.asarray(L0_attn_x, dtype=np.float32))
    wq = np.ascontiguousarray(np.asarray(L0_attn_W_query, dtype=np.float32))
    wk = np.ascontiguousarray(np.asarray(L0_attn_W_key, dtype=np.float32))
    wv = np.ascontiguousarray(np.asarray(L0_attn_W_value, dtype=np.float32))
    w1 = np.ascontiguousarray(np.asarray(L0_ffn_W1, dtype=np.float32))
    b1 = np.ascontiguousarray(np.asarray(L0_ffn_b1, dtype=np.float32))
    w2 = np.ascontiguousarray(np.asarray(L0_ffn_W2, dtype=np.float32))
    b2 = np.ascontiguousarray(np.asarray(L0_ffn_b2, dtype=np.float32))

    if "nc" not in _NC_CACHE:
        _NC_CACHE["nc"] = build_program()
    nc = _NC_CACHE["nc"]

    in_maps = [
        {"x": x[c], "wq": wq, "wk": wk, "wv": wv,
         "w1": w1, "b1": b1, "w2": w2, "b2": b2}
        for c in range(N_CORES)
    ]
    res = run_bass_kernel_spmd(nc, in_maps, core_ids=list(range(N_CORES)))
    return np.stack([np.asarray(res.results[c]["out"], dtype=np.float32)
                     for c in range(N_CORES)], axis=0)
